# version 17
# baseline (speedup 1.0000x reference)
import numpy as np
import concourse.bass as bass
import concourse.mybir as mybir
from contextlib import ExitStack
from concourse.bass_utils import run_bass_kernel_spmd

F32 = mybir.dt.float32
BF16 = mybir.dt.bfloat16
I32 = mybir.dt.int32
AF = mybir.ActivationFunctionType
ALU = mybir.AluOpType
AX = mybir.AxisListType

D = 2048
TCORE = 2048
HEADS = 32
DH = 64
PMS = 16
HID = 5632
FC = 16
HC = 44
NG = 4
GT = 512
SCALE = 0.125
EPS_LN = 1e-5
EPS_RMS = float(np.finfo(np.float32).eps)
NEG = -1e30
INV_D = 1.0 / D

XG = 0
R1AO = 32768
QF = 65536
VT = 98304
GUHI = 32768
GULO = 77824
KF = 131072
H2HI = 131072
H2LO = 147456
ZF = 131072
SCR = 163840
XCF = 172032
WREG = 180224
XCSQ = 180224
SROW = 192512
IDENT = 196608
MASKE = 197120
MASKO = 198144
PMK = 199168
PMV = 200192
MODV = 208384
SGSU = 209408
STATV = 209920
RROW = 210432
MISC = 210944
ARENA_TOP = 212736

MV_SA, MV_CA1, MV_GA, MV_SB, MV_CB1, MV_GB = 0, 16, 32, 48, 64, 80
MV_W2, MV_SFWN, MV_SACA, MV_WN, MV_SDGB, MV_WSAB = 96, 112, 128, 144, 160, 176
MV_C = 192
MV_EPSLN = 193
MV_EPSRMS = 194
MV_ONE = 195
MV_T0 = 200

ST_MU, ST_RSTD, ST_RRMS, ST_SC1, ST_T0, ST_T1, ST_T2, ST_T3 = range(8)


class Sem:
    def __init__(self, handle):
        self.h = handle
        self.n = 0


class Builder:
    ENGINES = ("sync", "gpsimd", "tensor", "vector", "scalar")

    def __init__(self):
        self.nc = bass.Bass("TRN2", target_bir_lowering=False)
        self.stack = ExitStack()
        self.ops = {e: [] for e in self.ENGINES}
        self.guards = {}

    def put(self, eng, fn):
        self.ops[eng].append(fn)

    def wait(self, eng, sem, val):
        if val is None or val <= 0:
            return
        self.put(eng, lambda e, s=sem.h, v=val: e.wait_ge(s, v))

    def op(self, eng, builder, inc=None, k=1):
        if eng in ("vector", "scalar"):
            self.put(eng, lambda e: e.drain())
        if inc is None:
            self.put(eng, builder)
            return None
        inc.n += k
        v = inc.n
        self.put(eng, lambda e, b=builder, s=inc.h, kk=k: b(e).then_inc(s, kk))
        return v

    def dma(self, eng, out, in_, inc):
        return self.op(eng, lambda e, o=out, i=in_: e.dma_start(out=o, in_=i),
                       inc=inc, k=16)

    def gset(self, key, sem, val):
        self.guards[key] = (sem, val)

    def gwait(self, eng, key):
        g = self.guards.get(key)
        if g is not None:
            self.wait(eng, g[0], g[1])

    WKEYS = ([("wreg",), ("wreg2",)] + [("wt", i) for i in range(3)]
             + [("wts", i) for i in range(3)] + [("vts", i) for i in range(3)]
             + [("vt", i) for i in range(3)] + [("gw", i) for i in range(3)]
             + [("uw", i) for i in range(3)] + [("dw", i) for i in range(4)])

    def wreg_barrier(self, eng):
        for k in self.WKEYS:
            g = self.guards.pop(k, None)
            if g is not None:
                self.wait(eng, g[0], g[1])

    def flush(self):
        with self.nc.Block() as b:
            for name in self.ENGINES:
                fns = self.ops[name]
                if fns:
                    getattr(b, name)(lambda e, fns=tuple(fns): [f(e) for f in fns])
        self.ops = {e: [] for e in self.ENGINES}

    def build(self):
        nc = self.nc
        st = self.stack
        self.d_x = nc.dram_tensor("x", [TCORE, D], F32, kind="ExternalInput").ap()
        self.d_out = nc.dram_tensor("out", [TCORE, D], F32, kind="ExternalOutput").ap()
        self.d_wqkv = nc.dram_tensor("wqkv", [48, D, 128], F32, kind="ExternalInput").ap()
        self.d_wout = nc.dram_tensor("wout", [16, D, 128], F32, kind="ExternalInput").ap()
        self.d_wmod = nc.dram_tensor("wmod", [24, D, 512], F32, kind="ExternalInput").ap()
        self.d_wg = nc.dram_tensor("wg", [HC, D, 128], I32, kind="ExternalInput").ap()
        self.d_wu = nc.dram_tensor("wu", [HC, D, 128], I32, kind="ExternalInput").ap()
        self.d_wd = nc.dram_tensor("wd", [16, HID, 128], I32, kind="ExternalInput").ap()
        self.d_wqh = nc.dram_tensor("wqh", [48, D, 128], BF16, kind="Internal").ap()
        self.d_wql = nc.dram_tensor("wql", [48, D, 128], BF16, kind="Internal").ap()
        self.d_woh = nc.dram_tensor("woh", [16, D, 128], BF16, kind="Internal").ap()
        self.d_wol = nc.dram_tensor("wol", [16, D, 128], BF16, kind="Internal").ap()
        self.d_pmk = nc.dram_tensor("pmkv", [2, HEADS, PMS, DH], F32, kind="ExternalInput").ap()
        self.d_c = nc.dram_tensor("crow", [16, 128], F32, kind="ExternalInput").ap()
        self.d_wn = nc.dram_tensor("wnorm", [16, 128], F32, kind="ExternalInput").ap()
        self.d_sg = nc.dram_tensor("sgate", [HC, 128], F32, kind="ExternalInput").ap()
        self.d_su = nc.dram_tensor("sup", [HC, 128], F32, kind="ExternalInput").ap()
        self.d_sd = nc.dram_tensor("sdown", [16, 128], F32, kind="ExternalInput").ap()
        self.d_id = nc.dram_tensor("ident", [128, 128], F32, kind="ExternalInput").ap()
        self.d_me = nc.dram_tensor("maske", [128, 512], BF16, kind="ExternalInput").ap()
        self.d_mo = nc.dram_tensor("masko", [128, 512], BF16, kind="ExternalInput").ap()

        self.arena = st.enter_context(nc.sbuf_tensor("arena", [128, ARENA_TOP // 4], F32))
        self.A = self.arena.ap()
        self.pbt = [st.enter_context(nc.psum_tensor("pb%d" % i, [128, 512], F32))
                    for i in range(8)]
        self.PB = [t.ap() for t in self.pbt]

        for name in ("sMM", "sV", "sA", "sP0", "sPM", "sCS", "sPL"):
            setattr(self, name, Sem(st.enter_context(nc.semaphore(name))))
        mk = lambda nm, n: [Sem(st.enter_context(nc.semaphore(nm + str(i))))
                            for i in range(n)]
        self.sWM = [mk("sWM" + str(b), 4) for b in range(3)]
        self.sX = mk("sX", 4)
        self.sWT = mk("sWT", 3)
        self.sVT = mk("sVT", 3)
        self.sGW = mk("sGW", 3)
        self.sUW = mk("sUW", 3)
        self.sDW = mk("sDW", 4)
        self.sC = mk("sC", 2)
        self.sWS = mk("sWS", 2)
        self.sST = mk("sST", 4)
        self.sQS = mk("sQS", 3)
        self.sVS = mk("sVS", 3)

        A = self.A

        def vf(off, *shape):
            n = int(np.prod(shape))
            ap = A[:, off // 4: off // 4 + n]
            if len(shape) == 2:
                ap = ap.rearrange("p (a b) -> p a b", b=shape[1])
            elif len(shape) == 3:
                ap = ap.rearrange("p (a b c) -> p a b c", b=shape[1], c=shape[2])
            return ap

        def vb(off, *shape):
            n = int(np.prod(shape))
            ap = A[:, off // 4: off // 4 + (n + 1) // 2].bitcast(BF16)[:, :n]
            if len(shape) == 2:
                ap = ap.rearrange("p (a b) -> p a b", b=shape[1])
            elif len(shape) == 3:
                ap = ap.rearrange("p (a b c) -> p a b c", b=shape[1], c=shape[2])
            return ap

        self.vf, self.vb = vf, vb
        self.xg = vf(XG, 4, D)
        self.r1hi = vb(R1AO, FC, GT)
        self.r1lo = vb(R1AO + 16384, FC, GT)
        self.aohi = vb(XG, FC, GT)
        self.aolo = vb(XG + 16384, FC, GT)
        self.qf = vf(QF, FC, GT)
        self.kf = vf(KF, FC, GT)
        self.vtm = vf(VT, 4, D)
        self.opj = vf(KF, FC, GT)
        self.zf = vf(ZF, FC, GT)
        self.h2hi = vb(H2HI, FC, GT)
        self.h2lo = vb(H2LO, FC, GT)
        self.guhi = vb(GUHI, HC, GT)
        self.gulo = vb(GULO, HC, GT)
        self.xc = vf(SCR, D)
        self.xcf = vf(XCF, FC, 128)
        self.xcsq = vf(XCSQ, FC, 128)
        self.sqdump = vf(XCSQ, D)
        self.ident = vf(IDENT, 128)
        self.maske = vb(MASKE, 512)
        self.masko = vb(MASKO, 512)
        self.pmk = vf(PMK, 16, PMS)
        self.pmv = vf(PMV, HEADS, DH)
        self.modv = vf(MODV, 256)
        self.sgsu = vf(SGSU, 128)
        self.statv = vf(STATV, 128)
        self.rrow_bf = A[0:1, (RROW + 256) // 4: (RROW + 256) // 4 + 64].bitcast(BF16)
        self.saca_row = A[0:1, SROW // 4: SROW // 4 + 1024].bitcast(BF16)
        self.row_a = A[0:1, MISC // 4: MISC // 4 + 128]
        self.row_b = A[0:1, (MISC + 1152) // 4: (MISC + 1152) // 4 + 128]
        self.d1t = vf(MISC + 512, 128)
        self.sct = vf(MISC + 1024, 16)
        self.negm = vf(MISC + 1088, 12)
        self.identb = A[:, RROW // 4: RROW // 4 + 64].bitcast(BF16)
        self.smb = [vf(SCR + 0, 4, 128), vf(SCR + 4096, 4, 128)]
        self.pnb = [vf(SCR + 2048, 4, 128), vf(SCR + 6144, 4, 128)]
        self.ptsb = [vf(SCR + 8192, 4, 128), vf(SCR + 10240, 4, 128)]
        self.sgate = vf(SCR + 0, 512)
        self.guf = vf(SCR + 2048, 512)
        self.tmp512 = vf(XCF, 4, 128)

        stop_after = getattr(self, "stop_after", None)

        def done(tag):
            return stop_after is not None and tag == stop_after

        aborted = False
        self.phase_marks = []
        for tag, fn in [("p0", self.p0_consts),
                        ("p2", self.p2_mod)]:
            self.phase_marks.append((tag, len(self.ops["tensor"])))
            fn()
            if done(tag):
                aborted = True
                break
        if not aborted:
            for g in range(NG):
                for tag, fn in [("b1", self.b1_ln1), ("b2", self.b2_qkv),
                                ("b3", self.b3_attn), ("b4", self.b4_outproj),
                                ("b5", self.b5_resid), ("b6", self.b6_ln2),
                                ("b7", self.b7_ffn)]:
                    self.phase_marks.append((tag + str(g), len(self.ops["tensor"])))
                    fn(g)
                    if g == 0 and done(tag):
                        aborted = True
                        break
                if aborted:
                    break
        if aborted:
            for t in range(4):
                self.dma("sync", self.d_out[t * 128:(t + 1) * 128, :],
                         self.xg[:, t, :], self.sST[t])
        for t in range(4):
            self.wait("sync", self.sST[t], self.sST[t].n)
        self.flush()
        return self.nc

    def mv(self, c0, n=16):
        return self.modv[:, c0:c0 + n]

    def mv1(self, c0):
        return self.modv[:, c0:c0 + 1]

    def stv(self, gi, slot, n=1):
        return self.statv[:, gi * 8 + slot: gi * 8 + slot + n]

    def p0_consts(self):
        S, A = self, self.A
        sP0, sPM, sMM, sV, sA_ = S.sP0, S.sPM, S.sMM, S.sV, S.sA
        PB, ident = S.PB, S.ident
        lds = [
            S.dma("sync", ident, S.d_id[:, :], sP0),
            S.dma("sync", S.maske, S.d_me[:, :], sP0),
            S.dma("sync", S.masko, S.d_mo[:, :], sP0),
        ]
        w = WREG // 4
        t_wn = A[0:16, w: w + 128]
        t_sg = A[0:HC, w + 128: w + 256]
        t_su = A[0:HC, w + 256: w + 384]
        t_sd = A[0:16, w + 384: w + 512]
        t_c = A[0:16, w + 512: w + 640]
        t_pmk = A[:, w + 640: w + 896].rearrange("p (a b) -> p a b", b=64)
        t_pmv = A[:, w + 896: w + 1152].rearrange("p (a b) -> p a b", b=64)
        t_pmkT = A[0:64, w + 1152: w + 1664]
        lds.append(S.dma("sync", t_wn, S.d_wn[:, :], sP0))
        lds.append(S.dma("sync", t_sg, S.d_sg[:, :], sP0))
        lds.append(S.dma("sync", t_su, S.d_su[:, :], sP0))
        lds.append(S.dma("sync", t_sd, S.d_sd[:, :], sP0))
        lds.append(S.dma("sync", t_c, S.d_c[:, :], sP0))
        lds.append(S.dma("sync", t_pmk,
                         S.d_pmk[0].rearrange("h p d -> (h p) d")
                         .rearrange("(a q) d -> q a d", q=128), sP0))
        lds.append(S.dma("sync", t_pmv,
                         S.d_pmk[1].rearrange("h p d -> (h p) d")
                         .rearrange("(a q) d -> q a d", q=128), sP0))
        S.wait("tensor", sP0, max(lds))
        S.op("tensor", lambda e: e.matmul(PB[6][:, 0:16], t_wn, ident[0:16, 0:16],
                                          is_transpose=True, start=True, stop=False))
        S.op("tensor", lambda e: e.matmul(PB[6][:, 16:60], t_sg, ident[0:HC, 0:HC],
                                          is_transpose=True, start=False, stop=False))
        S.op("tensor", lambda e: e.matmul(PB[6][:, 60:104], t_su, ident[0:HC, 0:HC],
                                          is_transpose=True, start=False, stop=False))
        S.op("tensor", lambda e: e.matmul(PB[6][:, 104:120], t_sd, ident[0:16, 0:16],
                                          is_transpose=True, start=False, stop=False))
        m0 = S.op("tensor", lambda e: e.matmul(PB[6][:, 120:136], t_c, ident[0:16, 0:16],
                                               is_transpose=True, start=False, stop=True),
                  inc=sMM)
        for g in range(4):
            S.op("tensor", lambda e, g=g: e.matmul(
                PB[7][0:64, g * 128:(g + 1) * 128], t_pmk[:, g, :], ident[:, :],
                is_transpose=True, start=(g == 0), stop=(g == 3)),
                 inc=(sMM if g == 3 else None))
        m1 = sMM.n

        S.wait("vector", sMM, m0)
        S.op("vector", lambda e: e.tensor_copy(S.mv(MV_WN), PB[6][:, 0:16]))
        S.op("vector", lambda e: e.tensor_copy(S.sgsu[:, 0:44], PB[6][:, 16:60]))
        S.op("vector", lambda e: e.tensor_copy(S.sgsu[:, 44:88], PB[6][:, 60:104]))
        S.op("vector", lambda e: e.tensor_copy(S.sgsu[:, 88:104], PB[6][:, 104:120]))
        ev0 = S.op("vector", lambda e: e.tensor_copy(S.sct, PB[6][:, 120:136]), inc=sV)
        S.wait("scalar", sV, ev0)
        self.a_silu_c = S.op("scalar", lambda e: e.activation(out=S.sct, in_=S.sct,
                                                              func=AF.Silu), inc=sA_)
        S.wait("vector", sMM, m1)
        ev1 = S.op("vector", lambda e: e.tensor_copy(t_pmkT, PB[7][0:64, :]), inc=sV)
        S.wait("sync", sV, ev1)
        pmdmas = []
        pmkv = S.vf(PMK, 16, PMS)
        for h in range(HEADS):
            c0 = (h // 8) * 128 + (h % 8) * 16
            src = t_pmkT[0:64, c0:c0 + 16]
            r0 = (h % 2) * 64
            pmdmas.append(S.dma("sync", pmkv[r0:r0 + 64, h // 2, :], src, sPM))
        for h in range(HEADS):
            src = t_pmv[(h % 8) * 16:(h % 8) * 16 + 16, h // 8, :]
            pmdmas.append(S.dma("sync", S.pmv[0:16, h, :], src, sPM))
            pmdmas.append(S.dma("sync", S.pmv[64:80, h, :], src, sPM))
        self.pm_done = max(pmdmas)
        S.op("vector", lambda e: e.memset(S.mv1(MV_EPSLN), EPS_LN))
        S.op("vector", lambda e: e.memset(S.mv1(MV_EPSRMS), EPS_RMS))
        self.v_consts = S.op("vector", lambda e: e.memset(S.mv1(MV_ONE), 1.0), inc=sV)
        S.wait("vector", sP0, max(lds))
        self.identb_done = S.op("vector", lambda e: e.tensor_copy(S.identb, ident),
                                inc=sV)
        S.gset(("wreg",), S.sPM, self.pm_done)

    def p2_mod(self):
        S = self
        PB, ident = S.PB, S.ident
        sMM, sV, sA_ = S.sMM, S.sV, S.sA
        wmb = [S.vf(XG, FC, 512), S.vf(R1AO, FC, 512), S.vf(QF, FC, 512)]
        qeng = ["sync", "gpsimd", "scalar"]
        msrow = S.A[0:1, (SCR + 8192) // 4: (SCR + 8192) // 4 + 512]
        S.wait("tensor", sA_, self.a_silu_c)
        for n in range(24):
            bi = n % 3
            eng = qeng[bi]
            S.gwait(eng, ("wmb", bi))
            src = S.d_wmod[n].rearrange("(a q) w -> q a w", q=128)
            lds2 = [S.dma(eng, wmb[bi][:, fc, :], src[:, fc, :], S.sWM[bi][fc // 4])
                    for fc in range(FC)]
            S.gwait("tensor", ("pb", 4))
            for fc in range(FC):
                if fc % 4 == 0:
                    S.wait("tensor", S.sWM[bi][fc // 4], lds2[fc + 3])
                S.op("tensor", lambda e, bi=bi, fc=fc: e.matmul(
                    PB[4][0:1, :], S.sct[:, fc:fc + 1], wmb[bi][:, fc, :],
                    start=(fc == 0), stop=(fc == 15)),
                     inc=(sMM if fc == 15 else None))
            mmn = sMM.n
            S.gset(("wmb", bi), sMM, mmn)
            S.wait("vector", sMM, mmn)
            evm = S.op("vector", lambda e: e.tensor_copy(msrow, PB[4][0:1, :]), inc=sV)
            S.gset(("pb", 4), sV, evm)
            S.wait("tensor", sV, evm)
            S.gwait("tensor", ("pb", 5))
            for jj in range(4):
                S.op("tensor", lambda e, jj=jj: e.matmul(
                    PB[5][:, jj:jj + 1], msrow[:, jj * 128:(jj + 1) * 128],
                    ident[0:1, 0:1], is_transpose=True,
                    start=(jj == 0), stop=(jj == 3)),
                     inc=(sMM if jj == 3 else None))
            mmt = sMM.n
            S.wait("vector", sMM, mmt)
            vec, fc0 = n // 4, 4 * (n % 4)
            dst = S.modv[:, vec * 16 + fc0: vec * 16 + fc0 + 4]
            evq = S.op("vector", lambda e, d=dst: e.tensor_copy(d, PB[5][:, 0:4]),
                       inc=sV)
            S.gset(("pb", 5), sV, evq)
        S.wait("vector", S.sPM, self.pm_done)
        mv, mv1 = S.mv, S.mv1

        def vop(fn, inc=None):
            return S.op("vector", fn, inc=inc)

        vop(lambda e: e.tensor_scalar_add(mv(MV_CA1), mv(MV_CA1), 1.0))
        vop(lambda e: e.tensor_scalar_add(mv(MV_CB1), mv(MV_CB1), 1.0))
        for c in (MV_GA, MV_GB):
            g = mv(c)
            tpos, tneg, tmask = mv(MV_T0), mv(MV_T0 + 16), mv(MV_T0 + 32)
            vop(lambda e, g=g, t=tpos: e.tensor_scalar_max(t, g, 0.25))
            vop(lambda e, g=g, t=tneg: e.tensor_scalar_min(t, g, -0.25))
            vop(lambda e, g=g, t=tmask: e.tensor_single_scalar(t, g, 0.0, ALU.is_ge))
            vop(lambda e, a=tpos, b=tneg: e.tensor_sub(a, a, b))
            vop(lambda e, m=tmask, a=tpos: e.tensor_mul(a, a, m))
            vop(lambda e, g=g, a=tpos, b=tneg: e.tensor_add(g, a, b))
        vop(lambda e: e.tensor_mul(mv(MV_W2), mv(MV_CA1), mv(MV_CA1)))
        vop(lambda e: e.tensor_mul(mv(MV_SFWN), mv(MV_CA1), mv(MV_WN)))
        vop(lambda e: e.tensor_mul(mv(MV_WSAB), mv(MV_CA1), mv(MV_SA)))
        vop(lambda e: e.reciprocal(mv(MV_T0), mv(MV_CA1)))
        vop(lambda e: e.tensor_mul(mv(MV_SACA), mv(MV_SA), mv(MV_T0)))
        vop(lambda e: e.tensor_mul(mv(MV_SDGB), S.sgsu[:, 88:104], mv(MV_GB)))
        vop(lambda e: e.tensor_mul(mv(MV_T0), mv(MV_SA), mv(MV_SA)))
        vC = vop(lambda e: e.reduce_sum(mv1(MV_T0 + 16), mv(MV_T0), axis=AX.X),
                 inc=sV)
        S.wait("tensor", sV, vC)
        S.gwait("tensor", ("pb", 4))
        mC = S.op("tensor", lambda e: e.matmul(PB[4][0:1, 0:1], mv1(MV_ONE),
                                               mv1(MV_T0 + 16), start=True,
                                               stop=True), inc=sMM)
        S.wait("vector", sMM, mC)
        vC2 = S.op("vector", lambda e: e.tensor_copy(msrow[:, 0:1], PB[4][0:1, 0:1]),
                   inc=sV)
        vOr = S.op("vector", lambda e: e.memset(msrow[:, 1:129], 1.0), inc=sV)
        S.wait("tensor", sV, vOr)
        mC2 = S.op("tensor", lambda e: e.matmul(PB[4][:, 1:2], msrow[:, 1:129],
                                                msrow[:, 0:1], start=True,
                                                stop=True), inc=sMM)
        S.wait("vector", sMM, mC2)
        vC3 = S.op("vector", lambda e: e.tensor_copy(mv1(MV_C), PB[4][:, 1:2]),
                   inc=sV)
        S.gset(("pb", 4), sV, vC3)
        S.gwait("tensor", ("pb", 5))
        mS = S.op("tensor", lambda e: e.matmul(PB[5][0:16, 0:128], mv(MV_SACA),
                                               ident, is_transpose=True,
                                               start=True, stop=True), inc=sMM)
        t16 = S.A[0:16, (SCR + 12288) // 4: (SCR + 12288) // 4 + 128]
        S.wait("vector", sMM, mS)
        vS = S.op("vector", lambda e: e.tensor_copy(t16, PB[5][0:16, 0:128]), inc=sV)
        S.gset(("pb", 5), sV, vS)
        S.wait("gpsimd", S.sV, vS)
        self.saca_done = S.dma("gpsimd",
                               self.saca_row.rearrange("p (a b) -> p a b", b=128),
                               t16, S.sCS)
        S.gset(("scr",), S.sCS, self.saca_done)
        self.mod_done = vC3

    def b1_ln1(self, g):
        S = self
        PB, ident = S.PB, S.ident
        sMM, sV, sA_ = S.sMM, S.sV, S.sA
        xcs = [S.xc, S.vf(KF, D)]

        def stage_a(i):
            gi = 4 * g + i
            if g == 0 and i == 0:
                S.gwait("sync", ("wmb", 0))
            S.gwait("sync", ("xg", i))
            vx = S.dma("sync", S.xg[:, i, :],
                       S.d_x[gi * 128:(gi + 1) * 128, :], S.sX[i])
            xt = S.xg[:, i, :]
            if i == 0:
                S.wreg_barrier("scalar")
                if g == 0:
                    S.wait("tensor", S.sV, self.mod_done)
                    S.wait("scalar", S.sV, self.v_consts)
            S.gwait("scalar", ("wreg",))
            S.wait("scalar", S.sX[i], vx)
            aq = S.op("scalar", lambda e, xt=xt, gi=gi: e.activation(
                out=S.sqdump, in_=xt, func=AF.Square,
                accum_out=S.stv(gi, ST_T0)), inc=sA_)
            S.gset(("wreg",), sA_, aq)
            S.wait("vector", S.sX[i], vx)
            S.op("vector", lambda e, xt=xt, gi=gi: e.reduce_sum(
                S.stv(gi, ST_T1), xt, axis=AX.X))
            S.op("vector", lambda e, gi=gi: e.tensor_scalar_mul(
                S.stv(gi, ST_MU), S.stv(gi, ST_T1), INV_D))
            S.wait("vector", sA_, aq)
            S.op("vector", lambda e, gi=gi: e.tensor_scalar_mul(
                S.stv(gi, ST_T2), S.stv(gi, ST_T0), INV_D))
            S.op("vector", lambda e, gi=gi: e.tensor_mul(
                S.stv(gi, ST_T3), S.stv(gi, ST_MU), S.stv(gi, ST_MU)))
            vv = S.op("vector", lambda e, gi=gi: e.tensor_sub(
                S.stv(gi, ST_T2), S.stv(gi, ST_T2), S.stv(gi, ST_T3)), inc=sV)
            S.wait("scalar", sV, vv)
            asq = S.op("scalar", lambda e, gi=gi: e.activation(
                out=S.stv(gi, ST_T3), in_=S.stv(gi, ST_T2), func=AF.Sqrt,
                bias=S.mv1(MV_EPSLN), scale=1.0), inc=sA_)
            S.wait("vector", sA_, asq)
            S.op("vector", lambda e, gi=gi: e.reciprocal(
                S.stv(gi, ST_RSTD), S.stv(gi, ST_T3)))
            xc = xcs[i % 2]
            if i % 2 == 0:
                S.gwait("vector", ("scr",))
            else:
                S.gwait("vector", ("zfree",))
            S.gwait("vector", ("xcs", i % 2))
            vxc = S.op("vector", lambda e, xt=xt, gi=gi, xc=xc: e.tensor_scalar(
                xc[:, :], xt, S.stv(gi, ST_MU), None, ALU.subtract), inc=sV)
            if i == 3:
                S.gset(("xgf_v",), sV, vxc)
                S.gset(("xgf_a",), sA_, sA_.n)
            return vxc

        def stage_b(i, vxc):
            gi = 4 * g + i
            xc = xcs[i % 2]
            evq = {}
            for q in range(4):
                bk = q % 2
                S.gwait("tensor", ("pb", bk))
                if q == 0:
                    S.wait("tensor", sV, vxc)
                for jj in range(4):
                    fc = 4 * q + jj
                    S.op("tensor", lambda e, bk=bk, jj=jj, fc=fc, xc=xc: e.matmul(
                        PB[bk][:, jj * 128:(jj + 1) * 128],
                        xc[:, fc * 128:(fc + 1) * 128], ident,
                        is_transpose=True, start=(jj == 0), stop=(jj == 3)),
                         inc=(sMM if jj == 3 else None))
                mq = sMM.n
                S.wait("vector", sMM, mq)
                if g == 0 and i == 0 and q == 0:
                    S.wait("vector", S.sCS, self.saca_done)
                evq[q] = S.op("vector", lambda e, bk=bk, q=q: e.tensor_copy(
                    S.xcf[:, 4 * q:4 * q + 4, :],
                    PB[bk].rearrange("p (a b) -> p a b", b=128)), inc=sV)
                S.gset(("pb", bk), sV, evq[q])
            S.gwait("scalar", ("wreg",))
            S.wait("scalar", sV, evq[3])
            asq2 = S.op("scalar", lambda e: e.activation(
                out=S.xcsq.rearrange("p a b -> p (a b)"),
                in_=S.xcf.rearrange("p a b -> p (a b)"), func=AF.Square), inc=sA_)
            S.gwait("tensor", ("pb", 2))
            S.gwait("tensor", ("pb", 3))
            S.wait("tensor", sA_, asq2)
            for fc in range(FC):
                S.op("tensor", lambda e, fc=fc: e.matmul(
                    PB[2][0:1, 0:128], S.mv(MV_W2)[:, fc:fc + 1],
                    S.xcsq[:, fc, :], start=(fc == 0), stop=(fc == 15)))
            for fc in range(FC):
                S.op("tensor", lambda e, fc=fc: e.matmul(
                    PB[3][0:1, 0:128], S.mv(MV_WSAB)[:, fc:fc + 1],
                    S.xcf[:, fc, :], start=(fc == 0), stop=(fc == 15)),
                     inc=(sMM if fc == 15 else None))
            mAB = sMM.n
            S.gset(("wreg",), sMM, mAB)
            S.wait("vector", sMM, mAB)
            S.gwait("vector", ("absb",))
            S.op("vector", lambda e: e.tensor_copy(S.row_a, PB[2][0:1, 0:128]))
            vab = S.op("vector", lambda e: e.tensor_copy(S.row_b, PB[3][0:1, 0:128]),
                       inc=sV)
            S.gset(("pb", 2), sV, vab)
            S.wait("tensor", sV, vab)
            S.op("tensor", lambda e: e.matmul(PB[3][:, 0:1], S.row_a,
                                              ident[0:1, 0:1], is_transpose=True,
                                              start=True, stop=True))
            mABt = S.op("tensor", lambda e: e.matmul(
                PB[3][:, 1:2], S.row_b, ident[0:1, 0:1], is_transpose=True,
                start=True, stop=True), inc=sMM)
            S.gset(("absb",), sMM, mABt)
            S.wait("vector", sMM, mABt)
            vabt = S.op("vector", lambda e, gi=gi: e.tensor_copy(
                S.stv(gi, ST_T0, 2), PB[3][:, 0:2]), inc=sV)
            S.gset(("pb", 3), sV, vabt)
            S.op("vector", lambda e, gi=gi: e.tensor_mul(
                S.stv(gi, ST_T2), S.stv(gi, ST_RSTD), S.stv(gi, ST_RSTD)))
            S.op("vector", lambda e, gi=gi: e.tensor_mul(
                S.stv(gi, ST_T2), S.stv(gi, ST_T2), S.stv(gi, ST_T0)))
            S.op("vector", lambda e, gi=gi: e.tensor_mul(
                S.stv(gi, ST_T3), S.stv(gi, ST_RSTD), S.stv(gi, ST_T1)))
            S.op("vector", lambda e, gi=gi: e.tensor_scalar_mul(
                S.stv(gi, ST_T3), S.stv(gi, ST_T3), 2.0))
            S.op("vector", lambda e, gi=gi: e.tensor_add(
                S.stv(gi, ST_T2), S.stv(gi, ST_T2), S.stv(gi, ST_T3)))
            S.op("vector", lambda e, gi=gi: e.tensor_add(
                S.stv(gi, ST_T2), S.stv(gi, ST_T2), S.mv1(MV_C)))
            vrr = S.op("vector", lambda e, gi=gi: e.tensor_scalar_mul(
                S.stv(gi, ST_T2), S.stv(gi, ST_T2), INV_D), inc=sV)
            S.wait("scalar", sV, vrr)
            ar = S.op("scalar", lambda e, gi=gi: e.activation(
                out=S.stv(gi, ST_T3), in_=S.stv(gi, ST_T2), func=AF.Sqrt,
                bias=S.mv1(MV_EPSRMS), scale=1.0), inc=sA_)
            S.wait("vector", sA_, ar)
            S.op("vector", lambda e, gi=gi: e.reciprocal(
                S.stv(gi, ST_RRMS), S.stv(gi, ST_T3)))
            vsc = S.op("vector", lambda e, gi=gi: e.tensor_mul(
                S.stv(gi, ST_SC1), S.stv(gi, ST_RSTD), S.stv(gi, ST_RRMS)), inc=sV)
            S.wait("tensor", sV, vsc)
            S.gwait("tensor", ("pb", 3))
            mrr = S.op("tensor", lambda e, gi=gi: e.matmul(
                PB[3][0:1, 0:128], S.stv(gi, ST_RRMS), ident,
                is_transpose=True, start=True, stop=True), inc=sMM)
            S.wait("vector", sMM, mrr)
            S.gwait("vector", ("rrow",))
            vrow = S.op("vector", lambda e: e.tensor_copy(
                S.rrow_bf, PB[3][0:1, 0:128]), inc=sV)
            S.gset(("pb", 3), sV, vrow)
            S.gwait("vector", ("d1t",))
            vd1 = S.op("vector", lambda e, gi=gi: e.tensor_scalar_mul(
                S.d1t, ident, S.stv(gi, ST_SC1)), inc=sV)
            if g == 0 and i == 0:
                S.wait("tensor", S.sCS, self.saca_done)
            for q in range(4):
                bk = q % 2
                S.gwait("tensor", ("pb", bk))
                if q == 0:
                    S.wait("tensor", sV, max(vd1, vrow))
                for jj in range(4):
                    fc = 4 * q + jj
                    S.op("tensor", lambda e, bk=bk, jj=jj, fc=fc, xc=xc: e.matmul(
                        PB[bk][:, jj * 128:(jj + 1) * 128],
                        xc[:, fc * 128:(fc + 1) * 128], S.d1t,
                        start=(jj == 0), stop=False))
                    S.op("tensor", lambda e, bk=bk, jj=jj, fc=fc: e.matmul(
                        PB[bk][:, jj * 128:(jj + 1) * 128],
                        S.saca_row[:, fc * 128:(fc + 1) * 128], S.rrow_bf,
                        start=False, stop=(jj == 3)),
                         inc=(sMM if jj == 3 else None))
                mq = sMM.n
                if q == 3:
                    S.gset(("xcs", i % 2), sMM, mq)
                    if i % 2 == 0:
                        S.gset(("scr",), sMM, mq)
                    else:
                        S.gset(("xc2",), sMM, mq)
                    S.gset(("d1t",), sMM, mq)
                    S.gset(("rrow",), sMM, mq)
                S.wait("vector", sMM, mq)
                if i == 0 and q == 0:
                    S.gwait("vector", ("gufree",))
                    if g == 0:
                        S.gwait("vector", ("wmb", 1))
                sf = S.mv(MV_SFWN)[:, 4 * q:4 * q + 4].unsqueeze(2) \
                    .broadcast_to((128, 4, 128))
                tmpf = S.xcf.rearrange("p a b -> p (a b)")[:, 0:512] \
                    .rearrange("p (a b) -> p a b", b=128)
                S.op("vector", lambda e, bk=bk, sf=sf, t=tmpf: e.tensor_mul(
                    t, PB[bk].rearrange("p (a b) -> p a b", b=128), sf))
                S.op("vector", lambda e, q=q, i=i, t=tmpf: e.tensor_copy(
                    S.r1hi[:, 4 * q:4 * q + 4, i * 128:(i + 1) * 128], t))
                evr = S.op("vector", lambda e, q=q, i=i, t=tmpf: e.tensor_tensor(
                    S.r1lo[:, 4 * q:4 * q + 4, i * 128:(i + 1) * 128], t,
                    S.r1hi[:, 4 * q:4 * q + 4, i * 128:(i + 1) * 128],
                    ALU.subtract), inc=sV)
                S.gset(("pb", bk), sV, evr)

        va = {0: stage_a(0), 1: stage_a(1)}
        stage_b(0, va[0])
        va[2] = stage_a(2)
        stage_b(1, va[1])
        va[3] = stage_a(3)
        stage_b(2, va[2])
        stage_b(3, va[3])
        self.r1_done = sV.n

    def b2_qkv(self, g):
        S = self
        PB = S.PB
        sMM, sV, sA_ = S.sMM, S.sV, S.sA
        S.wreg_barrier("sync")
        S.wait("tensor", sV, self.r1_done)

        def emit_qk_mms(bk, hiq, loq, j, fc, first, last, ready, qend):
            if ready is not None:
                S.wait("tensor", *ready)
            S.op("tensor", lambda e: e.matmul(
                PB[bk], hiq[:, j, :], S.r1hi[:, fc, :],
                start=first, stop=False))
            S.op("tensor", lambda e: e.matmul(
                PB[bk], hiq[:, j, :], S.r1lo[:, fc, :],
                start=False, stop=False))
            S.op("tensor", lambda e: e.matmul(
                PB[bk], loq[:, j, :], S.r1hi[:, fc, :],
                start=False, stop=last), inc=(sMM if qend else None))

        def stream_oc(oc, bk, d_f32, d_hi, d_lo, nsub):
            if g == 0:
                for qt in range(4):
                    slot = (self._wtc + qt) % 3
                    off = WREG + slot * 4096
                    f32q = S.vf(off, 4, 128)
                    hiq = S.vb(off + 2048, 4, 128)
                    loq = S.vb(off + 3072, 4, 128)
                    S.gwait("sync", ("wt", slot))
                    S.gwait("sync", ("wts", slot))
                    src = d_f32[oc].rearrange("(a q) w -> q a w", q=128)[:, qt * 4:qt * 4 + 4, :]
                    ld = S.dma("sync", f32q, src, S.sWT[slot])
                    S.gwait("vector", ("wt", slot))
                    S.wait("vector", S.sWT[slot], ld)
                    S.op("vector", lambda e, f=f32q, h=hiq: e.tensor_copy(h, f))
                    vsp = S.op("vector", lambda e, f=f32q, h=hiq, l=loq:
                               e.tensor_tensor(l, f, h, ALU.subtract), inc=sV)
                    S.wait("gpsimd", sV, vsp)
                    dh = d_hi[oc].rearrange("(a q) w -> q a w", q=128)[:, qt * 4:qt * 4 + 4, :]
                    dl = d_lo[oc].rearrange("(a q) w -> q a w", q=128)[:, qt * 4:qt * 4 + 4, :]
                    S.dma("gpsimd", dh, hiq, S.sQS[slot])
                    sq = S.dma("gpsimd", dl, loq, S.sQS[slot])
                    S.gset(("wts", slot), S.sQS[slot], sq)
                    for j in range(4):
                        fc = qt * 4 + j
                        emit_qk_mms(bk, hiq, loq, j, fc, fc == 0, fc == 15,
                                    (sV, vsp) if j == 0 else None, j == 3)
                    S.gset(("wt", slot), sMM, sMM.n)
                self._wtc += 4
            else:
                for hf in range(2):
                    slot = (self._wtc + hf) % 3
                    off = WREG + slot * 4096
                    hiq = S.vb(off, 8, 128)
                    loq = S.vb(off + 2048, 8, 128)
                    S.gwait("sync", ("wt", slot))
                    sh = d_hi[oc].rearrange("(a q) w -> q a w", q=128)[:, hf * 8:hf * 8 + 8, :]
                    sl = d_lo[oc].rearrange("(a q) w -> q a w", q=128)[:, hf * 8:hf * 8 + 8, :]
                    S.dma("sync", hiq, sh, S.sWT[slot])
                    ld = S.dma("sync", loq, sl, S.sWT[slot])
                    for j in range(8):
                        fc = hf * 8 + j
                        emit_qk_mms(bk, hiq, loq, j, fc, fc == 0, fc == 15,
                                    (S.sWT[slot], ld) if j == 0 else None, j == 7)
                    S.gset(("wt", slot), sMM, sMM.n)
                self._wtc += 2
            return sMM.n

        self._wtc = 0
        for oc in range(32):
            bk = oc % 3
            S.gwait("tensor", ("pb", bk))
            m = stream_oc(oc, bk, S.d_wqkv, S.d_wqh, S.d_wql, 4)
            if oc < 16:
                S.wait("vector", sMM, m)
                if oc == 0:
                    S.gwait("vector", ("gufree",))
                    S.gwait("vector", ("wmb", 2))
                    S.gwait("vector", ("xc6b",))
                ev = S.op("vector", lambda e, bk=bk, oc=oc: e.tensor_scalar_mul(
                    S.qf[:, oc, :], PB[bk], SCALE), inc=sV)
                S.gset(("pb", bk), sV, ev)
            else:
                S.wait("scalar", sMM, m)
                if oc == 16:
                    S.gwait("scalar", ("zfree",))
                    S.gwait("scalar", ("xc2",))
                ev = S.op("scalar", lambda e, bk=bk, oc=oc: e.activation(
                    out=S.kf[:, oc - 16, :], in_=PB[bk], func=AF.Copy), inc=sA_)
                S.gset(("pb", bk), sA_, ev)

        self.qkv_done_v = sV.n
        self.qkv_done_a = sA_.n

    def b3_attn(self, g):
        S = self
        PB, ident = S.PB, S.ident
        sMM, sV, sA_ = S.sMM, S.sV, S.sA
        S.gwait("vector", ("scr",))
        if g == 0:
            S.wait("tensor", S.sPM, self.pm_done)
        S.wait("tensor", S.sV, self.qkv_done_v)
        S.wait("tensor", S.sA, self.qkv_done_a)
        S.wreg_barrier("sync")

        vstep = {"n": 0}
        os_mark = {}

        def emit_v_step():
            n = vstep["n"]
            if n >= 128:
                return
            vstep["n"] = n + 1
            os_, pair, fc = n // 32, (n // 16) % 2, n % 16
            slot = n % 3
            off = WREG + slot * 4096
            hiq = S.vb(off + 2048, 4, 128)
            loq = S.vb(off + 3072, 4, 128)
            hiq_f = hiq.rearrange("p a b -> p (a b)")
            loq_f = loq.rearrange("p a b -> p (a b)")
            oc0 = 32 + 4 * os_
            if g == 0:
                f32q = S.vf(off, 4, 128)
                S.gwait("sync", ("vt", slot))
                S.gwait("sync", ("vts", slot))
                srcw = S.d_wqkv[oc0:oc0 + 4, fc * 128:(fc + 1) * 128, :] \
                    .transpose([1, 0, 2])
                ld = S.dma("sync", f32q, srcw, S.sVT[slot])
                S.gwait("vector", ("vt", slot))
                S.wait("vector", S.sVT[slot], ld)
                S.op("vector", lambda e, f=f32q, h=hiq: e.tensor_copy(h, f))
                vsp = S.op("vector", lambda e, f=f32q, h=hiq, l=loq:
                           e.tensor_tensor(l, f, h, ALU.subtract), inc=sV)
                if pair == 0:
                    S.wait("gpsimd", sV, vsp)
                    dh = S.d_wqh[oc0:oc0 + 4, fc * 128:(fc + 1) * 128, :] \
                        .transpose([1, 0, 2])
                    dl = S.d_wql[oc0:oc0 + 4, fc * 128:(fc + 1) * 128, :] \
                        .transpose([1, 0, 2])
                    S.dma("gpsimd", dh, hiq, S.sVS[slot])
                    sq = S.dma("gpsimd", dl, loq, S.sVS[slot])
                    S.gset(("vts", slot), S.sVS[slot], sq)
                ready = (sV, vsp)
            else:
                S.gwait("sync", ("vt", slot))
                sh = S.d_wqh[oc0:oc0 + 4, fc * 128:(fc + 1) * 128, :] \
                    .transpose([1, 0, 2])
                sl = S.d_wql[oc0:oc0 + 4, fc * 128:(fc + 1) * 128, :] \
                    .transpose([1, 0, 2])
                S.dma("sync", hiq, sh, S.sVT[slot])
                ld = S.dma("sync", loq, sl, S.sVT[slot])
                ready = (S.sVT[slot], ld)
            S.wait("tensor", *ready)
            for w in range(2):
                tt = 2 * pair + w
                bk = 6 + w
                if fc == 0:
                    S.gwait("tensor", ("pb", bk))
                S.op("tensor", lambda e, bk=bk, tt=tt, fc=fc, h=hiq_f: e.matmul(
                    PB[bk], S.r1hi[:, fc, tt * 128:(tt + 1) * 128], h,
                    start=(fc == 0), stop=False))
                S.op("tensor", lambda e, bk=bk, tt=tt, fc=fc, h=hiq_f: e.matmul(
                    PB[bk], S.r1lo[:, fc, tt * 128:(tt + 1) * 128], h,
                    start=False, stop=False))
                S.op("tensor", lambda e, bk=bk, tt=tt, fc=fc, l=loq_f: e.matmul(
                    PB[bk], S.r1hi[:, fc, tt * 128:(tt + 1) * 128], l,
                    start=False, stop=(fc == 15)),
                     inc=(sMM if w == 1 else None))
            S.gset(("vt", slot), sMM, sMM.n)
            if fc == 15:
                mv_ = sMM.n
                for w in range(2):
                    tt = 2 * pair + w
                    bk = 6 + w
                    eng, sem = (("vector", sV) if w == 0 else ("scalar", sA_))
                    S.wait(eng, sMM, mv_)
                    if os_ == 0 and pair == 0:
                        S.gwait(eng, ("gufree",))
                    if eng == "vector":
                        ev = S.op(eng, lambda e, bk=bk, tt=tt, os_=os_:
                                  e.tensor_copy(S.vtm[:, tt, os_ * 512:(os_ + 1) * 512],
                                                PB[bk]), inc=sem)
                    else:
                        ev = S.op(eng, lambda e, bk=bk, tt=tt, os_=os_:
                                  e.activation(out=S.vtm[:, tt, os_ * 512:(os_ + 1) * 512],
                                               in_=PB[bk], func=AF.Copy), inc=sem)
                    S.gset(("pb", bk), sem, ev)
                if pair == 1:
                    os_mark[os_] = (sV.n, sA_.n)
            if vstep["n"] == 128:
                S.gset(("r1free",), sMM, sMM.n)

        st = {}

        def emit_S(k):
            hc, par = k // 2, k % 2
            sb = k % 2
            sc = k % 2
            mask = S.maske if par == 0 else S.masko
            S.gwait("tensor", ("pb", sb))
            if g == 0 and k == 0:
                S.wait("tensor", sV, self.identb_done)
            for r in range(2):
                rh = r * 64
                S.op("tensor", lambda e, sb=sb, rh=rh, mask=mask: e.matmul(
                    PB[sb][rh:rh + 64, :], S.identb[rh:rh + 64, rh:rh + 64],
                    mask[rh:rh + 64, :], start=True, stop=False))
                for s in range(4):
                    w = 2 * s + par
                    Q = S.qf[rh:rh + 64, hc, w * 64:(w + 1) * 64]
                    K = S.kf[rh:rh + 64, hc, w * 64:(w + 1) * 64]
                    pk = S.pmk[rh:rh + 64, hc, :]
                    if par == 0:
                        c_tok, c_pm = s * 128, s * 128 + 64
                    else:
                        c_tok, c_pm = s * 128 + 64, s * 128
                    S.op("tensor", lambda e, sb=sb, rh=rh, c=c_tok, Q=Q, K=K:
                         e.matmul(PB[sb][rh:rh + 64, c:c + 64], Q, K,
                                  start=False, stop=False))
                    S.op("tensor", lambda e, sb=sb, rh=rh, c=c_pm, Q=Q, pk=pk,
                         l=(s == 3): e.matmul(PB[sb][rh:rh + 64, c:c + 16], Q, pk,
                                              start=False, stop=l),
                         inc=(sMM if (s == 3 and r == 1) else None))
            st[k] = {"mS": sMM.n, "sb": sb, "sc": sc}

        def emit_softmax(k):
            hc, par = k // 2, k % 2
            d = st[k]
            sb, sc = d["sb"], d["sc"]
            ns = 4 + 4 * (k % 2)
            S.gwait("scalar", ("smb", sc))
            S.wait("scalar", sMM, d["mS"])
            aE = None
            for s in range(4):
                aE = S.op("scalar", lambda e, sb=sb, sc=sc, s=s, ns=ns:
                          e.activation(out=S.pnb[sc][:, s, :],
                                       in_=PB[sb][:, s * 128:(s + 1) * 128],
                                       func=AF.Exp,
                                       accum_out=S.negm[:, ns + s:ns + s + 1]),
                          inc=(sA_ if s == 3 else None))
            S.gset(("pb", sb), sA_, aE)
            S.wait("vector", sA_, aE)
            S.op("vector", lambda e, ns=ns: e.reciprocal(
                S.negm[:, ns:ns + 4], S.negm[:, ns:ns + 4]))
            vPn = S.op("vector", lambda e, sc=sc, ns=ns: e.tensor_tensor(
                S.pnb[sc], S.pnb[sc],
                S.negm[:, ns:ns + 4].unsqueeze(2).broadcast_to((128, 4, 128)),
                ALU.mult), inc=sV)
            d["vPn"] = vPn

        def emit_ptav(k):
            hc, par = k // 2, k % 2
            d = st[k]
            sc = d["sc"]
            ptb = 2 + k % 2
            S.gwait("tensor", ("pb", ptb))
            S.wait("tensor", sV, d["vPn"])
            for s in range(4):
                S.op("tensor", lambda e, ptb=ptb, sc=sc, s=s:
                     e.matmul(PB[ptb][:, s * 128:(s + 1) * 128],
                              S.pnb[sc][:, s, :], ident,
                              is_transpose=True, start=True, stop=True),
                     inc=(sMM if s == 3 else None))
            mT = sMM.n
            S.gset(("smb", sc), sMM, mT)
            S.gwait("vector", ("pts", sc))
            S.wait("vector", sMM, mT)
            vPT = S.op("vector", lambda e, ptb=ptb, sc=sc: e.tensor_copy(
                S.ptsb[sc].rearrange("p a b -> p (a b)"), PB[ptb]), inc=sV)
            S.gset(("pb", ptb), sV, vPT)
            need = hc // 4
            mk = os_mark[need]
            S.wait("tensor", sV, max(vPT, mk[0]))
            S.wait("tensor", sA_, mk[1])
            if par == 0:
                S.gwait("tensor", ("pb", 4))
                S.gwait("tensor", ("pb", 5))
                self._ao_started = [False, False]
            for s in range(4):
                for r in range(2):
                    h = 2 * hc + r
                    w = 2 * s + par
                    cq = slice(r * 64, r * 64 + 64)
                    ab = 4 if r == 0 else 5
                    if par == 0:
                        tokL = S.vtm[0:64, w // 2, h * 64:(h + 1) * 64]
                        tokR = S.ptsb[sc][0:64, s, cq]
                        pmL = S.pmv[64:80, h, :]
                        pmR = S.ptsb[sc][64:80, s, cq]
                    else:
                        tokL = S.vtm[64:128, w // 2, h * 64:(h + 1) * 64]
                        tokR = S.ptsb[sc][64:128, s, cq]
                        pmL = S.pmv[0:16, h, :]
                        pmR = S.ptsb[sc][0:16, s, cq]
                    out = PB[ab][r * 64:r * 64 + 64, w * 64:(w + 1) * 64]
                    startf = not self._ao_started[r]
                    self._ao_started[r] = True
                    stopf = (par == 1 and s == 3)
                    S.op("tensor", lambda e, o=out, L=tokL, R=tokR, f=startf:
                         e.matmul(o, L, R, start=f, stop=False))
                    S.op("tensor", lambda e, o=out, L=pmL, R=pmR, stp=stopf:
                         e.matmul(o, L, R, start=False, stop=stp),
                         inc=(sMM if (s == 3 and r == 1) else None))
            mAV = sMM.n
            S.gset(("pts", sc), sMM, mAV)
            if par == 1:
                S.wait("vector", sMM, mAV)
                if hc == 0:
                    S.gwait("vector", ("xgf_v",))
                    S.gwait("vector", ("xgf_a",))
                S.op("vector", lambda e, hc=hc: e.tensor_copy(
                    S.aohi[0:64, hc, :], PB[4][0:64, :]))
                S.op("vector", lambda e, hc=hc: e.tensor_tensor(
                    S.aolo[0:64, hc, :], PB[4][0:64, :],
                    S.aohi[0:64, hc, :], ALU.subtract))
                S.op("vector", lambda e, hc=hc: e.tensor_copy(
                    S.aohi[64:128, hc, :], PB[5][64:128, :]))
                evAO = S.op("vector", lambda e, hc=hc: e.tensor_tensor(
                    S.aolo[64:128, hc, :], PB[5][64:128, :],
                    S.aohi[64:128, hc, :], ALU.subtract), inc=sV)
                S.gset(("pb", 4), sV, evAO)
                S.gset(("pb", 5), sV, evAO)

        for _ in range(32):
            emit_v_step()
        last_smm = None
        for k in range(33):
            if k < 32:
                if k < 24:
                    for _ in range(4):
                        emit_v_step()
                emit_S(k)
                last_smm = st[k]["mS"]
                emit_softmax(k)
            if k >= 1:
                emit_ptav(k - 1)
        S.gset(("kf",), sMM, last_smm)
        S.gset(("scr",), sMM, sMM.n)
        self.ao_done = self.sV.n

    def b4_outproj(self, g):
        S = self
        PB = S.PB
        sMM, sV = S.sMM, S.sV
        S.wreg_barrier("sync")
        S.wait("tensor", sV, self.ao_done)
        self._wtc = 0

        def emit_mms(bk, hiq, loq, j, fc, first, last, ready, qend):
            if ready is not None:
                S.wait("tensor", *ready)
            S.op("tensor", lambda e: e.matmul(
                PB[bk], hiq[:, j, :], S.aohi[:, fc, :], start=first, stop=False))
            S.op("tensor", lambda e: e.matmul(
                PB[bk], hiq[:, j, :], S.aolo[:, fc, :], start=False, stop=False))
            S.op("tensor", lambda e: e.matmul(
                PB[bk], loq[:, j, :], S.aohi[:, fc, :], start=False, stop=last),
                 inc=(sMM if qend else None))

        for oc in range(16):
            bk = oc % 2
            S.gwait("tensor", ("pb", bk))
            if g == 0:
                for qt in range(4):
                    slot = (self._wtc + qt) % 3
                    off = WREG + slot * 4096
                    f32q = S.vf(off, 4, 128)
                    hiq = S.vb(off + 2048, 4, 128)
                    loq = S.vb(off + 3072, 4, 128)
                    S.gwait("sync", ("wt", slot))
                    S.gwait("sync", ("wts", slot))
                    src = S.d_wout[oc].rearrange("(a q) w -> q a w", q=128)[:, qt * 4:qt * 4 + 4, :]
                    ld = S.dma("sync", f32q, src, S.sWT[slot])
                    S.gwait("vector", ("wt", slot))
                    S.wait("vector", S.sWT[slot], ld)
                    S.op("vector", lambda e, f=f32q, h=hiq: e.tensor_copy(h, f))
                    vsp = S.op("vector", lambda e, f=f32q, h=hiq, l=loq:
                               e.tensor_tensor(l, f, h, ALU.subtract), inc=sV)
                    S.wait("gpsimd", sV, vsp)
                    dh = S.d_woh[oc].rearrange("(a q) w -> q a w", q=128)[:, qt * 4:qt * 4 + 4, :]
                    dl = S.d_wol[oc].rearrange("(a q) w -> q a w", q=128)[:, qt * 4:qt * 4 + 4, :]
                    S.dma("gpsimd", dh, hiq, S.sQS[slot])
                    sq = S.dma("gpsimd", dl, loq, S.sQS[slot])
                    S.gset(("wts", slot), S.sQS[slot], sq)
                    for j in range(4):
                        fc = qt * 4 + j
                        emit_mms(bk, hiq, loq, j, fc, fc == 0, fc == 15,
                                 (sV, vsp) if j == 0 else None, j == 3)
                    S.gset(("wt", slot), sMM, sMM.n)
                self._wtc += 4
            else:
                for hf in range(2):
                    slot = (self._wtc + hf) % 3
                    off = WREG + slot * 4096
                    hiq = S.vb(off, 8, 128)
                    loq = S.vb(off + 2048, 8, 128)
                    S.gwait("sync", ("wt", slot))
                    sh = S.d_woh[oc].rearrange("(a q) w -> q a w", q=128)[:, hf * 8:hf * 8 + 8, :]
                    sl = S.d_wol[oc].rearrange("(a q) w -> q a w", q=128)[:, hf * 8:hf * 8 + 8, :]
                    S.dma("sync", hiq, sh, S.sWT[slot])
                    ld = S.dma("sync", loq, sl, S.sWT[slot])
                    for j in range(8):
                        fc = hf * 8 + j
                        emit_mms(bk, hiq, loq, j, fc, fc == 0, fc == 15,
                                 (S.sWT[slot], ld) if j == 0 else None, j == 7)
                    S.gset(("wt", slot), sMM, sMM.n)
                self._wtc += 2
            m = sMM.n
            S.wait("vector", sMM, m)
            if oc == 0:
                S.gwait("vector", ("kf",))
            ev = S.op("vector", lambda e, bk=bk, oc=oc: e.tensor_scalar(
                S.opj[:, oc, :], PB[bk], S.mv(MV_GA)[:, oc:oc + 1], None,
                ALU.mult), inc=sV)
            S.gset(("pb", bk), sV, ev)
        self.opj_done = sV.n
        S.gset(("aofree",), sMM, sMM.n)

    def b5_resid(self, g):
        S = self
        PB, ident = S.PB, S.ident
        sMM, sV = S.sMM, S.sV
        self.x1_tile = [None] * 4
        xrt = S.vf(XCF, D)
        for tt in range(4):
            gi = 4 * g + tt
            if tt == 0:
                S.gwait("sync", ("scr",))
                S.gwait("sync", ("scr2",))
            S.gwait("sync", ("xrt",))
            vx = S.dma("sync", xrt, S.d_x[gi * 128:(gi + 1) * 128, :], S.sX[tt])
            for q in range(4):
                bk = 2 + (tt * 4 + q) % 2
                S.gwait("tensor", ("pb", bk))
                S.wait("tensor", sV, self.opj_done)
                for jj in range(4):
                    oc = 4 * q + jj
                    S.op("tensor", lambda e, bk=bk, jj=jj, oc=oc, tt=tt: e.matmul(
                        PB[bk][:, jj * 128:(jj + 1) * 128],
                        S.opj[:, oc, tt * 128:(tt + 1) * 128], ident,
                        is_transpose=True, start=(jj == 0), stop=(jj == 3)),
                         inc=(sMM if jj == 3 else None))
                m = sMM.n
                S.wait("vector", sMM, m)
                if q == 0:
                    S.wait("vector", S.sX[tt], vx)
                    if tt == 0:
                        S.gwait("vector", ("aofree",))
                ev = S.op("vector", lambda e, bk=bk, tt=tt, q=q: e.tensor_tensor(
                    S.xg[:, tt, q * 512:(q + 1) * 512], PB[bk],
                    xrt[:, q * 512:(q + 1) * 512], ALU.add), inc=sV)
                S.gset(("pb", bk), sV, ev)
            self.x1_tile[tt] = sV.n
            S.gset(("xrt",), sV, sV.n)
        S.gset(("kf2",), sMM, sMM.n)
        self.x1_done = sV.n

    def b6_ln2(self, g):
        S = self
        PB, ident = S.PB, S.ident
        sMM, sV, sA_ = S.sMM, S.sV, S.sA
        xcs6 = [S.xc, S.vf(QF, D)]
        d1s6 = [S.d1t, S.vf(QF + 8192, 128)]
        for i in range(4):
            gi = 4 * g + i
            xt = S.xg[:, i, :]
            sl = i % 2
            xc6, d16 = xcs6[sl], d1s6[sl]
            if i == 0:
                S.wreg_barrier("scalar")
                S.gwait("vector", ("kf2",))
            S.wait("scalar", S.sV, self.x1_tile[i])
            S.gwait("scalar", ("wreg",))
            aq = S.op("scalar", lambda e, xt=xt, gi=gi: e.activation(
                out=S.sqdump, in_=xt, func=AF.Square,
                accum_out=S.stv(gi, ST_T0)), inc=sA_)
            S.gset(("wreg",), sA_, aq)
            S.op("vector", lambda e, xt=xt, gi=gi: e.reduce_sum(
                S.stv(gi, ST_T1), xt, axis=AX.X))
            S.op("vector", lambda e, gi=gi: e.tensor_scalar_mul(
                S.stv(gi, ST_MU), S.stv(gi, ST_T1), INV_D))
            S.wait("vector", sA_, aq)
            S.op("vector", lambda e, gi=gi: e.tensor_scalar_mul(
                S.stv(gi, ST_T2), S.stv(gi, ST_T0), INV_D))
            S.op("vector", lambda e, gi=gi: e.tensor_mul(
                S.stv(gi, ST_T3), S.stv(gi, ST_MU), S.stv(gi, ST_MU)))
            vv = S.op("vector", lambda e, gi=gi: e.tensor_sub(
                S.stv(gi, ST_T2), S.stv(gi, ST_T2), S.stv(gi, ST_T3)), inc=sV)
            S.wait("scalar", sV, vv)
            asq = S.op("scalar", lambda e, gi=gi: e.activation(
                out=S.stv(gi, ST_T3), in_=S.stv(gi, ST_T2), func=AF.Sqrt,
                bias=S.mv1(MV_EPSLN), scale=1.0), inc=sA_)
            S.wait("vector", sA_, asq)
            S.op("vector", lambda e, gi=gi: e.reciprocal(
                S.stv(gi, ST_RSTD), S.stv(gi, ST_T3)))
            if i == 0:
                S.gwait("vector", ("scr",))
            elif i == 1:
                S.gwait("vector", ("kf",))
            S.gwait("vector", ("xc6", sl))
            vxc = S.op("vector", lambda e, xt=xt, gi=gi, xc6=xc6: e.tensor_scalar(
                xc6[:, :], xt, S.stv(gi, ST_MU), None, ALU.subtract), inc=sV)
            if sl == 0:
                S.gwait("vector", ("d1t",))
            vd1 = S.op("vector", lambda e, gi=gi, d16=d16: e.tensor_scalar_mul(
                d16, ident, S.stv(gi, ST_RSTD)), inc=sV)
            for q in range(4):
                bk = q % 2
                S.gwait("tensor", ("pb", bk))
                if q == 0:
                    S.wait("tensor", sV, max(vxc, vd1))
                for jj in range(4):
                    fc = 4 * q + jj
                    S.op("tensor", lambda e, bk=bk, jj=jj, fc=fc, xc6=xc6,
                         d16=d16: e.matmul(
                        PB[bk][:, jj * 128:(jj + 1) * 128],
                        xc6[:, fc * 128:(fc + 1) * 128], d16,
                        start=(jj == 0), stop=(jj == 3)),
                         inc=(sMM if jj == 3 else None))
                mq = sMM.n
                if q == 3:
                    S.gset(("xc6", sl), sMM, mq)
                    if sl == 0:
                        S.gset(("scr",), sMM, mq)
                        S.gset(("d1t",), sMM, mq)
                    else:
                        S.gset(("xc6b",), sMM, mq)
                S.wait("vector", sMM, mq)
                cb = S.mv(MV_CB1)[:, 4 * q:4 * q + 4].unsqueeze(2) \
                    .broadcast_to((128, 4, 128))
                sb = S.mv(MV_SB)[:, 4 * q:4 * q + 4].unsqueeze(2) \
                    .broadcast_to((128, 4, 128))
                S.gwait("vector", ("tmp512",))
                S.op("vector", lambda e, bk=bk, cb=cb: e.tensor_tensor(
                    S.tmp512, PB[bk].rearrange("p (a b) -> p a b", b=128),
                    cb, ALU.mult))
                vh = S.op("vector", lambda e, sb=sb: e.tensor_tensor(
                    S.tmp512, S.tmp512, sb, ALU.add), inc=sV)
                S.gset(("pb", bk), sV, vh)
                S.op("vector", lambda e, q=q, i=i: e.tensor_copy(
                    S.h2hi[:, 4 * q:4 * q + 4, i * 128:(i + 1) * 128], S.tmp512))
                vlo = S.op("vector", lambda e, q=q, i=i: e.tensor_tensor(
                    S.h2lo[:, 4 * q:4 * q + 4, i * 128:(i + 1) * 128], S.tmp512,
                    S.h2hi[:, 4 * q:4 * q + 4, i * 128:(i + 1) * 128],
                    ALU.subtract), inc=sV)
                S.gset(("tmp512",), sV, vlo)
        self.h2_done = sV.n

    def b7_ffn(self, g):
        S = self
        PB, ident = S.PB, S.ident
        sMM, sV, sA_ = S.sMM, S.sV, S.sA
        gw = [S.vb(WREG + k * 2048, 8, 128) for k in range(3)]
        uw = [S.vb(WREG + 6144 + k * 2048, 8, 128) for k in range(3)]
        S.wreg_barrier("gpsimd")
        S.wait("tensor", S.sV, self.h2_done)
        onesbf = S.vb(SCR + 8448, 1)
        n128row = S.A[0:1, (SCR + 8192) // 4: (SCR + 8192) // 4 + 64].bitcast(BF16)
        csrow = S.A[0:1, (SCR + 4096) // 4: (SCR + 4096) // 4 + 512]
        cshi = S.A[0:1, (SCR + 6144) // 4: (SCR + 6144) // 4 + 256].bitcast(BF16)
        cslo = S.A[0:1, (SCR + 7168) // 4: (SCR + 7168) // 4 + 256].bitcast(BF16)
        S.gwait("vector", ("scr",))
        S.op("vector", lambda e: e.memset(onesbf, 1.0))
        vcn = S.op("vector", lambda e: e.memset(n128row, -128.0), inc=sV)
        S.gset(("scr",), sV, vcn)
        S.gwait("tensor", ("pb", 6))
        S.wait("tensor", sV, vcn)
        for fc in range(FC):
            S.op("tensor", lambda e, fc=fc: e.matmul(
                PB[6][0:1, :], onesbf, S.h2hi[:, fc, :],
                start=(fc == 0), stop=False))
            S.op("tensor", lambda e, fc=fc: e.matmul(
                PB[6][0:1, :], onesbf, S.h2lo[:, fc, :],
                start=False, stop=(fc == 15)), inc=(sMM if fc == 15 else None))
        mcs = sMM.n
        S.wait("vector", sMM, mcs)
        S.op("vector", lambda e: e.tensor_copy(csrow, PB[6][0:1, :]))
        S.op("vector", lambda e: e.tensor_copy(cshi, csrow))
        vcs = S.op("vector", lambda e: e.tensor_tensor(
            cslo, csrow, cshi, ALU.subtract), inc=sV)
        S.gset(("pb", 6), sV, vcs)
        S.wait("tensor", sV, vcs)
        for hid in range(HC):
            lds = {}
            for hf in range(2):
                slot = (hid * 2 + hf) % 3
                S.gwait("gpsimd", ("gw", slot))
                S.gwait("gpsimd", ("uw", slot))
                lds[("g", hf)] = (slot, S.dma(
                    "gpsimd", gw[slot],
                    S.d_wg[hid].rearrange("(a q) w -> q a w", q=128)[:, hf * 8:hf * 8 + 8, :], S.sGW[slot]))
                lds[("u", hf)] = (slot, S.dma(
                    "gpsimd", uw[slot],
                    S.d_wu[hid].rearrange("(a q) w -> q a w", q=128)[:, hf * 8:hf * 8 + 8, :], S.sUW[slot]))
            bg, bu = hid % 2, 2 + hid % 2
            S.gwait("tensor", ("pb", bg))
            S.gwait("tensor", ("pb", bu))
            for fc in range(FC):
                hf = fc // 8
                slot = (hid * 2 + hf) % 3
                if fc % 8 == 0:
                    S.wait("tensor", S.sGW[lds[("g", hf)][0]], lds[("g", hf)][1])
                    S.wait("tensor", S.sUW[lds[("u", hf)][0]], lds[("u", hf)][1])
                S.op("tensor", lambda e, bg=bg, slot=slot, fc=fc: e.matmul(
                    PB[bg], gw[slot][:, fc % 8, :], S.h2hi[:, fc, :],
                    start=(fc == 0), stop=False))
                S.op("tensor", lambda e, bg=bg, slot=slot, fc=fc: e.matmul(
                    PB[bg], gw[slot][:, fc % 8, :], S.h2lo[:, fc, :],
                    start=False, stop=False))
                S.op("tensor", lambda e, bu=bu, slot=slot, fc=fc: e.matmul(
                    PB[bu], uw[slot][:, fc % 8, :], S.h2hi[:, fc, :],
                    start=(fc == 0), stop=False))
                S.op("tensor", lambda e, bu=bu, slot=slot, fc=fc: e.matmul(
                    PB[bu], uw[slot][:, fc % 8, :], S.h2lo[:, fc, :],
                    start=False, stop=False))
            S.op("tensor", lambda e, bg=bg: e.matmul(
                PB[bg], n128row, cshi, start=False, stop=False))
            S.op("tensor", lambda e, bg=bg: e.matmul(
                PB[bg], n128row, cslo, start=False, stop=True))
            S.op("tensor", lambda e, bu=bu: e.matmul(
                PB[bu], n128row, cshi, start=False, stop=False))
            S.op("tensor", lambda e, bu=bu: e.matmul(
                PB[bu], n128row, cslo, start=False, stop=True), inc=sMM)
            mg = sMM.n
            S.gset(("gw", (hid * 2) % 3), sMM, mg)
            S.gset(("gw", (hid * 2 + 1) % 3), sMM, mg)
            S.gset(("uw", (hid * 2) % 3), sMM, mg)
            S.gset(("uw", (hid * 2 + 1) % 3), sMM, mg)
            S.gwait("scalar", ("sgate",))
            if hid == 0:
                S.gwait("scalar", ("scr",))
                S.gwait("vector", ("scr",))
                S.gwait("vector", ("aofree",))
            S.wait("scalar", sMM, mg)
            as_ = S.op("scalar", lambda e, bg=bg, hid=hid: e.activation(
                out=S.sgate, in_=PB[bg], func=AF.Silu,
                scale=S.sgsu[:, hid:hid + 1]), inc=sA_)
            S.gset(("pb", bg), sA_, as_)
            S.wait("vector", sA_, as_)
            S.gwait("vector", ("guf",))
            S.op("vector", lambda e, bu=bu, hid=hid: e.scalar_tensor_tensor(
                S.guf, PB[bu], S.sgsu[:, 44 + hid:45 + hid], S.sgate,
                ALU.mult, ALU.mult))
            vg1 = S.op("vector", lambda e, hid=hid: e.tensor_copy(
                S.guhi[:, hid, :], S.guf), inc=sV)
            S.gset(("sgate",), sV, vg1)
            S.gset(("pb", bu), sV, vg1)
            vg2 = S.op("vector", lambda e, hid=hid: e.tensor_tensor(
                S.gulo[:, hid, :], S.guf, S.guhi[:, hid, :], ALU.subtract), inc=sV)
            S.gset(("guf",), sV, vg2)
        S.gset(("h2free",), sMM, sMM.n)
        self.gu_done = sV.n
        dw = [S.vb(WREG + k * 2880, 11, 128) for k in range(4)]
        S.wreg_barrier("gpsimd")
        S.wait("tensor", sV, self.gu_done)
        S.gwait("tensor", ("pb", 6))
        for hid in range(HC):
            S.op("tensor", lambda e, hid=hid: e.matmul(
                PB[6][0:1, :], onesbf, S.guhi[:, hid, :],
                start=(hid == 0), stop=False))
            S.op("tensor", lambda e, hid=hid: e.matmul(
                PB[6][0:1, :], onesbf, S.gulo[:, hid, :],
                start=False, stop=(hid == HC - 1)),
                 inc=(sMM if hid == HC - 1 else None))
        mcsd = sMM.n
        S.wait("vector", sMM, mcsd)
        S.op("vector", lambda e: e.tensor_copy(csrow, PB[6][0:1, :]))
        S.op("vector", lambda e: e.tensor_copy(cshi, csrow))
        vcsd = S.op("vector", lambda e: e.tensor_tensor(
            cslo, csrow, cshi, ALU.subtract), inc=sV)
        S.gset(("pb", 6), sV, vcsd)
        S.wait("tensor", sV, vcsd)
        zdone = {}
        for oc in range(16):
            lds = {}
            for qq in range(4):
                slot = (oc * 4 + qq) % 4
                S.gwait("gpsimd", ("dw", slot))
                lds[qq] = (slot, S.dma(
                    "gpsimd", dw[slot],
                    S.d_wd[oc, qq * 1408:(qq + 1) * 1408, :]
                    .rearrange("(a q) w -> q a w", q=128), S.sDW[slot]))
            bk = 4 + oc % 2
            S.gwait("tensor", ("pb", bk))
            for hid in range(HC):
                qq = hid // 11
                slot = (oc * 4 + qq) % 4
                if hid % 11 == 0:
                    S.wait("tensor", S.sDW[lds[qq][0]], lds[qq][1])
                S.op("tensor", lambda e, bk=bk, slot=slot, hid=hid: e.matmul(
                    PB[bk], dw[slot][:, hid % 11, :], S.guhi[:, hid, :],
                    start=(hid == 0), stop=False))
                S.op("tensor", lambda e, bk=bk, slot=slot, hid=hid: e.matmul(
                    PB[bk], dw[slot][:, hid % 11, :], S.gulo[:, hid, :],
                    start=False, stop=False),
                     inc=(sMM if hid % 11 == 10 else None))
                if hid % 11 == 10:
                    S.gset(("dw", slot), sMM, sMM.n)
            S.op("tensor", lambda e, bk=bk: e.matmul(
                PB[bk], n128row, cshi, start=False, stop=False))
            S.op("tensor", lambda e, bk=bk: e.matmul(
                PB[bk], n128row, cslo, start=False, stop=True), inc=sMM)
            md = sMM.n
            S.wait("vector", sMM, md)
            if oc == 0:
                S.gwait("vector", ("h2free",))
            ev = S.op("vector", lambda e, bk=bk, oc=oc: e.tensor_scalar(
                S.zf[:, oc, :], PB[bk], S.mv(MV_SDGB)[:, oc:oc + 1], None,
                ALU.mult), inc=sV)
            S.gset(("pb", bk), sV, ev)
            zdone[oc] = ev
        for tt in range(4):
            gi = 4 * g + tt
            evs = []
            for q in range(4):
                bk = 2 + (tt * 4 + q) % 2
                S.gwait("tensor", ("pb", bk))
                S.wait("tensor", sV, zdone[4 * q + 3])
                for jj in range(4):
                    oc = 4 * q + jj
                    S.op("tensor", lambda e, bk=bk, jj=jj, oc=oc, tt=tt: e.matmul(
                        PB[bk][:, jj * 128:(jj + 1) * 128],
                        S.zf[:, oc, tt * 128:(tt + 1) * 128], ident,
                        is_transpose=True, start=(jj == 0), stop=(jj == 3)),
                         inc=(sMM if jj == 3 else None))
                m = sMM.n
                S.wait("vector", sMM, m)
                ev = S.op("vector", lambda e, bk=bk, tt=tt, q=q: e.tensor_tensor(
                    S.xg[:, tt, q * 512:(q + 1) * 512], PB[bk],
                    S.xg[:, tt, q * 512:(q + 1) * 512], ALU.add), inc=sV)
                S.gset(("pb", bk), sV, ev)
                evs.append(ev)
            S.wait("sync", sV, max(evs))
            so = S.dma("sync", S.d_out[gi * 128:(gi + 1) * 128, :],
                       S.xg[:, tt, :], S.sST[tt])
            S.gset(("xg", tt), S.sST[tt], so)
        S.gset(("zfree",), sMM, sMM.n)
        S.gset(("gufree",), sMM, sMM.n)
        S.gset(("scr",), sMM, sMM.n)
        S.gset(("scr2",), sMM, sMM.n)


def kernel(L0_attn_seq, L0_attn_w_norm, L0_attn_w_qkv, L0_attn_w_out, L0_attn_pm,
           L0_ffn_wq_gate, L0_ffn_s_gate, L0_ffn_wq_up, L0_ffn_s_up,
           L0_ffn_wq_down, L0_ffn_s_down, L0_c, L0_wmod):
    import ml_dtypes

    x = np.asarray(L0_attn_seq, dtype=np.float32)
    wqkv = np.ascontiguousarray(
        np.asarray(L0_attn_w_qkv, dtype=np.float32).reshape(D, 48, 128)
        .transpose(1, 0, 2))
    wout = np.ascontiguousarray(
        np.asarray(L0_attn_w_out, dtype=np.float32).reshape(D, 16, 128)
        .transpose(1, 0, 2))
    wmod = np.ascontiguousarray(
        np.asarray(L0_wmod, dtype=np.float32).reshape(D, 24, 512)
        .transpose(1, 0, 2))
    wg = np.ascontiguousarray(
        np.asarray(L0_ffn_wq_gate, dtype=np.int32).T.reshape(D, HC, 128)
        .transpose(1, 0, 2))
    wu = np.ascontiguousarray(
        np.asarray(L0_ffn_wq_up, dtype=np.int32).T.reshape(D, HC, 128)
        .transpose(1, 0, 2))
    wd = np.ascontiguousarray(
        np.asarray(L0_ffn_wq_down, dtype=np.int32).T.reshape(HID, 16, 128)
        .transpose(1, 0, 2))
    pm = np.ascontiguousarray(np.asarray(L0_attn_pm, dtype=np.float32))
    c = np.asarray(L0_c, dtype=np.float32)
    wn = np.ascontiguousarray(np.asarray(L0_attn_w_norm, dtype=np.float32)
                              .reshape(16, 128))
    sg = np.ascontiguousarray(np.asarray(L0_ffn_s_gate, dtype=np.float32)
                              .reshape(HC, 128))
    su = np.ascontiguousarray(np.asarray(L0_ffn_s_up, dtype=np.float32)
                              .reshape(HC, 128))
    sd = np.ascontiguousarray(np.asarray(L0_ffn_s_down, dtype=np.float32)
                              .reshape(16, 128))
    ident = np.eye(128, dtype=np.float32)

    me = np.full((64, 128), NEG, dtype=np.float32)
    mo = np.full((64, 128), NEG, dtype=np.float32)
    for i in range(64):
        for t in range(64):
            if PMS + t <= i:
                me[i, t] = 0.0
                mo[i, 64 + t] = 0.0
        for p in range(PMS):
            if p <= i:
                me[i, 64 + p] = 0.0
                mo[i, p] = 0.0
    maske = np.ascontiguousarray(np.tile(me, (2, 4))).astype(ml_dtypes.bfloat16)
    masko = np.ascontiguousarray(np.tile(mo, (2, 4))).astype(ml_dtypes.bfloat16)

    b = Builder()
    nc = b.build()

    in_maps = []
    for core in range(8):
        bi = core // 4
        t0 = (core % 4) * TCORE
        in_maps.append({
            "x": np.ascontiguousarray(x[bi, t0:t0 + TCORE, :]),
            "wqkv": wqkv, "wout": wout, "wmod": wmod,
            "wg": wg, "wu": wu, "wd": wd,
            "pmkv": pm,
            "crow": np.ascontiguousarray(c[bi].reshape(16, 128)),
            "wnorm": wn, "sgate": sg, "sup": su, "sdown": sd,
            "ident": ident, "maske": maske, "masko": masko,
        })
    res = run_bass_kernel_spmd(nc, in_maps, core_ids=list(range(8)))
    out = np.empty((2, 8192, D), dtype=np.float32)
    for core in range(8):
        bi = core // 4
        t0 = (core % 4) * TCORE
        out[bi, t0:t0 + TCORE, :] = res.results[core]["out"]
    return out


# version 18
# speedup vs baseline: 1.0516x; 1.0516x over previous
import numpy as np
import concourse.bass as bass
import concourse.mybir as mybir
from contextlib import ExitStack
from concourse.bass_utils import run_bass_kernel_spmd

F32 = mybir.dt.float32
BF16 = mybir.dt.bfloat16
I32 = mybir.dt.int32
AF = mybir.ActivationFunctionType
ALU = mybir.AluOpType
AX = mybir.AxisListType

D = 2048
TCORE = 2048
HEADS = 32
DH = 64
PMS = 16
HID = 5632
FC = 16
HC = 44
NG = 4
GT = 512
SCALE = 0.125
EPS_LN = 1e-5
EPS_RMS = float(np.finfo(np.float32).eps)
NEG = -1e30
INV_D = 1.0 / D

XG = 0
R1AO = 32768
QF = 65536
VT = 98304
GUHI = 32768
GULO = 77824
KF = 131072
H2HI = 131072
H2LO = 147456
ZF = 131072
SCR = 163840
XCF = 172032
WREG = 180224
XCSQ = 180224
SROW = 192512
IDENT = 196608
MASKE = 197120
MASKO = 198144
PMK = 199168
PMV = 200192
MODV = 208384
SGSU = 209408
STATV = 209920
RROW = 210432
MISC = 210944
ARENA_TOP = 212736

MV_SA, MV_CA1, MV_GA, MV_SB, MV_CB1, MV_GB = 0, 16, 32, 48, 64, 80
MV_W2, MV_SFWN, MV_SACA, MV_WN, MV_SDGB, MV_WSAB = 96, 112, 128, 144, 160, 176
MV_C = 192
MV_EPSLN = 193
MV_EPSRMS = 194
MV_ONE = 195
MV_T0 = 200

ST_MU, ST_RSTD, ST_RRMS, ST_SC1, ST_T0, ST_T1, ST_T2, ST_T3 = range(8)


class Sem:
    def __init__(self, handle):
        self.h = handle
        self.n = 0


class Builder:
    ENGINES = ("sync", "gpsimd", "tensor", "vector", "scalar")

    def __init__(self):
        self.nc = bass.Bass("TRN2", target_bir_lowering=False)
        self.stack = ExitStack()
        self.ops = {e: [] for e in self.ENGINES}
        self.guards = {}

    def put(self, eng, fn):
        self.ops[eng].append(fn)

    def wait(self, eng, sem, val):
        if val is None or val <= 0:
            return
        self.put(eng, lambda e, s=sem.h, v=val: e.wait_ge(s, v))

    def op(self, eng, builder, inc=None, k=1):
        if eng in ("vector", "scalar"):
            self.put(eng, lambda e: e.drain())
        if inc is None:
            self.put(eng, builder)
            return None
        inc.n += k
        v = inc.n
        self.put(eng, lambda e, b=builder, s=inc.h, kk=k: b(e).then_inc(s, kk))
        return v

    def dma(self, eng, out, in_, inc):
        return self.op(eng, lambda e, o=out, i=in_: e.dma_start(out=o, in_=i),
                       inc=inc, k=16)

    def gset(self, key, sem, val):
        self.guards[key] = (sem, val)

    def gwait(self, eng, key):
        g = self.guards.get(key)
        if g is not None:
            self.wait(eng, g[0], g[1])

    WKEYS = ([("wreg",), ("wreg2",)] + [("wt", i) for i in range(3)]
             + [("wts", i) for i in range(3)] + [("vts", i) for i in range(3)]
             + [("vt", i) for i in range(3)] + [("gw", i) for i in range(3)]
             + [("uw", i) for i in range(3)] + [("dw", i) for i in range(4)])

    def wreg_barrier(self, eng):
        for k in self.WKEYS:
            g = self.guards.pop(k, None)
            if g is not None:
                self.wait(eng, g[0], g[1])

    def flush(self):
        with self.nc.Block() as b:
            for name in self.ENGINES:
                fns = self.ops[name]
                if fns:
                    getattr(b, name)(lambda e, fns=tuple(fns): [f(e) for f in fns])
        self.ops = {e: [] for e in self.ENGINES}

    def build(self):
        nc = self.nc
        st = self.stack
        self.d_x = nc.dram_tensor("x", [TCORE, D], F32, kind="ExternalInput").ap()
        self.d_out = nc.dram_tensor("out", [TCORE, D], F32, kind="ExternalOutput").ap()
        self.d_wqkv = nc.dram_tensor("wqkv", [48, D, 128], F32, kind="ExternalInput").ap()
        self.d_wout = nc.dram_tensor("wout", [16, D, 128], F32, kind="ExternalInput").ap()
        self.d_wmod = nc.dram_tensor("wmod", [24, D, 512], F32, kind="ExternalInput").ap()
        self.d_wg = nc.dram_tensor("wg", [HC, D, 128], I32, kind="ExternalInput").ap()
        self.d_wu = nc.dram_tensor("wu", [HC, D, 128], I32, kind="ExternalInput").ap()
        self.d_wd = nc.dram_tensor("wd", [16, HID, 128], I32, kind="ExternalInput").ap()
        self.d_wqh = nc.dram_tensor("wqh", [48, D, 128], BF16, kind="Internal").ap()
        self.d_wql = nc.dram_tensor("wql", [48, D, 128], BF16, kind="Internal").ap()
        self.d_woh = nc.dram_tensor("woh", [16, D, 128], BF16, kind="Internal").ap()
        self.d_wol = nc.dram_tensor("wol", [16, D, 128], BF16, kind="Internal").ap()
        self.d_pmk = nc.dram_tensor("pmkv", [2, HEADS, PMS, DH], F32, kind="ExternalInput").ap()
        self.d_c = nc.dram_tensor("crow", [16, 128], F32, kind="ExternalInput").ap()
        self.d_wn = nc.dram_tensor("wnorm", [16, 128], F32, kind="ExternalInput").ap()
        self.d_sg = nc.dram_tensor("sgate", [HC, 128], F32, kind="ExternalInput").ap()
        self.d_su = nc.dram_tensor("sup", [HC, 128], F32, kind="ExternalInput").ap()
        self.d_sd = nc.dram_tensor("sdown", [16, 128], F32, kind="ExternalInput").ap()
        self.d_id = nc.dram_tensor("ident", [128, 128], F32, kind="ExternalInput").ap()
        self.d_me = nc.dram_tensor("maske", [128, 512], BF16, kind="ExternalInput").ap()
        self.d_mo = nc.dram_tensor("masko", [128, 512], BF16, kind="ExternalInput").ap()

        self.arena = st.enter_context(nc.sbuf_tensor("arena", [128, ARENA_TOP // 4], F32))
        self.A = self.arena.ap()
        self.pbt = [st.enter_context(nc.psum_tensor("pb%d" % i, [128, 512], F32))
                    for i in range(8)]
        self.PB = [t.ap() for t in self.pbt]

        for name in ("sMM", "sV", "sA", "sP0", "sPM", "sCS"):
            setattr(self, name, Sem(st.enter_context(nc.semaphore(name))))
        mk = lambda nm, n: [Sem(st.enter_context(nc.semaphore(nm + str(i))))
                            for i in range(n)]
        self.sWM = [mk("sWM" + str(b), 4) for b in range(3)]
        self.sX = mk("sX", 4)
        self.sWT = mk("sWT", 3)
        self.sVT = mk("sVT", 3)
        self.sGW = mk("sGW", 3)
        self.sUW = mk("sUW", 3)
        self.sDW = mk("sDW", 4)
        self.sC = mk("sC", 2)
        self.sWS = mk("sWS", 2)
        self.sST = mk("sST", 4)
        self.sQS = mk("sQS", 3)
        self.sVS = mk("sVS", 3)

        A = self.A

        def vf(off, *shape):
            n = int(np.prod(shape))
            ap = A[:, off // 4: off // 4 + n]
            if len(shape) == 2:
                ap = ap.rearrange("p (a b) -> p a b", b=shape[1])
            elif len(shape) == 3:
                ap = ap.rearrange("p (a b c) -> p a b c", b=shape[1], c=shape[2])
            return ap

        def vb(off, *shape):
            n = int(np.prod(shape))
            ap = A[:, off // 4: off // 4 + (n + 1) // 2].bitcast(BF16)[:, :n]
            if len(shape) == 2:
                ap = ap.rearrange("p (a b) -> p a b", b=shape[1])
            elif len(shape) == 3:
                ap = ap.rearrange("p (a b c) -> p a b c", b=shape[1], c=shape[2])
            return ap

        self.vf, self.vb = vf, vb
        self.xg = vf(XG, 4, D)
        self.r1hi = vb(R1AO, FC, GT)
        self.r1lo = vb(R1AO + 16384, FC, GT)
        self.aohi = vb(XG, FC, GT)
        self.aolo = vb(XG + 16384, FC, GT)
        self.qf = vf(QF, FC, GT)
        self.kf = vf(KF, FC, GT)
        self.vtm = vf(VT, 4, D)
        self.opj = vf(KF, FC, GT)
        self.zf = vf(ZF, FC, GT)
        self.h2hi = vb(H2HI, FC, GT)
        self.h2lo = vb(H2LO, FC, GT)
        self.guhi = vb(GUHI, HC, GT)
        self.gulo = vb(GULO, HC, GT)
        self.xc = vf(SCR, D)
        self.xcf = vf(XCF, FC, 128)
        self.xcsq = vf(XCSQ, FC, 128)
        self.sqdump = vf(XCSQ, D)
        self.ident = vf(IDENT, 128)
        self.maske = vb(MASKE, 512)
        self.masko = vb(MASKO, 512)
        self.pmk = vf(PMK, 16, PMS)
        self.pmv = vf(PMV, HEADS, DH)
        self.modv = vf(MODV, 256)
        self.sgsu = vf(SGSU, 128)
        self.statv = vf(STATV, 128)
        self.rrow_bf = A[0:1, (RROW + 256) // 4: (RROW + 256) // 4 + 64].bitcast(BF16)
        self.saca_row = A[0:1, SROW // 4: SROW // 4 + 1024].bitcast(BF16)
        self.row_a = A[0:1, MISC // 4: MISC // 4 + 128]
        self.row_b = A[0:1, (MISC + 1152) // 4: (MISC + 1152) // 4 + 128]
        self.d1t = vf(MISC + 512, 128)
        self.sct = vf(MISC + 1024, 16)
        self.negm = vf(MISC + 1088, 12)
        self.identb = A[:, RROW // 4: RROW // 4 + 64].bitcast(BF16)
        self.smb = [vf(SCR + 0, 4, 128), vf(SCR + 4096, 4, 128)]
        self.pnb = [vf(SCR + 0, 4, 128), vf(SCR + 2048, 4, 128),
                    vf(SCR + 4096, 4, 128)]
        self.ptsb = [vf(SCR + 8192, 4, 128), vf(SCR + 10240, 4, 128)]
        self.sgate = vf(SCR + 0, 512)
        self.guf = vf(SCR + 2048, 512)
        self.tmp512 = vf(XCF, 4, 128)

        stop_after = getattr(self, "stop_after", None)

        def done(tag):
            return stop_after is not None and tag == stop_after

        aborted = False
        self.phase_marks = []
        for tag, fn in [("p0", self.p0_consts),
                        ("p2", self.p2_mod)]:
            self.phase_marks.append((tag, len(self.ops["tensor"])))
            fn()
            if done(tag):
                aborted = True
                break
        if not aborted:
            for g in range(NG):
                for tag, fn in [("b1", self.b1_ln1), ("b2", self.b2_qkv),
                                ("b3", self.b3_attn), ("b4", self.b4_outproj),
                                ("b5", self.b5_resid), ("b6", self.b6_ln2),
                                ("b7", self.b7_ffn)]:
                    self.phase_marks.append((tag + str(g), len(self.ops["tensor"])))
                    fn(g)
                    if g == 0 and done(tag):
                        aborted = True
                        break
                if aborted:
                    break
        if aborted:
            for t in range(4):
                self.dma("sync", self.d_out[t * 128:(t + 1) * 128, :],
                         self.xg[:, t, :], self.sST[t])
        for t in range(4):
            self.wait("sync", self.sST[t], self.sST[t].n)
        self.flush()
        return self.nc

    def mv(self, c0, n=16):
        return self.modv[:, c0:c0 + n]

    def mv1(self, c0):
        return self.modv[:, c0:c0 + 1]

    def stv(self, gi, slot, n=1):
        return self.statv[:, gi * 8 + slot: gi * 8 + slot + n]

    def p0_consts(self):
        S, A = self, self.A
        sP0, sPM, sMM, sV, sA_ = S.sP0, S.sPM, S.sMM, S.sV, S.sA
        PB, ident = S.PB, S.ident
        lds = [
            S.dma("sync", ident, S.d_id[:, :], sP0),
            S.dma("sync", S.maske, S.d_me[:, :], sP0),
            S.dma("sync", S.masko, S.d_mo[:, :], sP0),
        ]
        w = WREG // 4
        t_wn = A[0:16, w: w + 128]
        t_sg = A[0:HC, w + 128: w + 256]
        t_su = A[0:HC, w + 256: w + 384]
        t_sd = A[0:16, w + 384: w + 512]
        t_c = A[0:16, w + 512: w + 640]
        t_pmk = A[:, w + 640: w + 896].rearrange("p (a b) -> p a b", b=64)
        t_pmv = A[:, w + 896: w + 1152].rearrange("p (a b) -> p a b", b=64)
        t_pmkT = A[0:64, w + 1152: w + 1664]
        lds.append(S.dma("sync", t_wn, S.d_wn[:, :], sP0))
        lds.append(S.dma("sync", t_sg, S.d_sg[:, :], sP0))
        lds.append(S.dma("sync", t_su, S.d_su[:, :], sP0))
        lds.append(S.dma("sync", t_sd, S.d_sd[:, :], sP0))
        lds.append(S.dma("sync", t_c, S.d_c[:, :], sP0))
        lds.append(S.dma("sync", t_pmk,
                         S.d_pmk[0].rearrange("h p d -> (h p) d")
                         .rearrange("(a q) d -> q a d", q=128), sP0))
        lds.append(S.dma("sync", t_pmv,
                         S.d_pmk[1].rearrange("h p d -> (h p) d")
                         .rearrange("(a q) d -> q a d", q=128), sP0))
        S.wait("tensor", sP0, max(lds))
        S.op("tensor", lambda e: e.matmul(PB[6][:, 0:16], t_wn, ident[0:16, 0:16],
                                          is_transpose=True, start=True, stop=False))
        S.op("tensor", lambda e: e.matmul(PB[6][:, 16:60], t_sg, ident[0:HC, 0:HC],
                                          is_transpose=True, start=False, stop=False))
        S.op("tensor", lambda e: e.matmul(PB[6][:, 60:104], t_su, ident[0:HC, 0:HC],
                                          is_transpose=True, start=False, stop=False))
        S.op("tensor", lambda e: e.matmul(PB[6][:, 104:120], t_sd, ident[0:16, 0:16],
                                          is_transpose=True, start=False, stop=False))
        m0 = S.op("tensor", lambda e: e.matmul(PB[6][:, 120:136], t_c, ident[0:16, 0:16],
                                               is_transpose=True, start=False, stop=True),
                  inc=sMM)
        for g in range(4):
            S.op("tensor", lambda e, g=g: e.matmul(
                PB[7][0:64, g * 128:(g + 1) * 128], t_pmk[:, g, :], ident[:, :],
                is_transpose=True, start=(g == 0), stop=(g == 3)),
                 inc=(sMM if g == 3 else None))
        m1 = sMM.n

        S.wait("vector", sMM, m0)
        S.op("vector", lambda e: e.tensor_copy(S.mv(MV_WN), PB[6][:, 0:16]))
        S.op("vector", lambda e: e.tensor_copy(S.sgsu[:, 0:44], PB[6][:, 16:60]))
        S.op("vector", lambda e: e.tensor_copy(S.sgsu[:, 44:88], PB[6][:, 60:104]))
        S.op("vector", lambda e: e.tensor_copy(S.sgsu[:, 88:104], PB[6][:, 104:120]))
        ev0 = S.op("vector", lambda e: e.tensor_copy(S.sct, PB[6][:, 120:136]), inc=sV)
        S.wait("scalar", sV, ev0)
        self.a_silu_c = S.op("scalar", lambda e: e.activation(out=S.sct, in_=S.sct,
                                                              func=AF.Silu), inc=sA_)
        S.wait("vector", sMM, m1)
        ev1 = S.op("vector", lambda e: e.tensor_copy(t_pmkT, PB[7][0:64, :]), inc=sV)
        S.wait("sync", sV, ev1)
        pmdmas = []
        pmkv = S.vf(PMK, 16, PMS)
        for h in range(HEADS):
            c0 = (h // 8) * 128 + (h % 8) * 16
            src = t_pmkT[0:64, c0:c0 + 16]
            r0 = (h % 2) * 64
            pmdmas.append(S.dma("sync", pmkv[r0:r0 + 64, h // 2, :], src, sPM))
        for h in range(HEADS):
            src = t_pmv[(h % 8) * 16:(h % 8) * 16 + 16, h // 8, :]
            pmdmas.append(S.dma("sync", S.pmv[0:16, h, :], src, sPM))
            pmdmas.append(S.dma("sync", S.pmv[64:80, h, :], src, sPM))
        self.pm_done = max(pmdmas)
        S.op("vector", lambda e: e.memset(S.mv1(MV_EPSLN), EPS_LN))
        S.op("vector", lambda e: e.memset(S.mv1(MV_EPSRMS), EPS_RMS))
        self.v_consts = S.op("vector", lambda e: e.memset(S.mv1(MV_ONE), 1.0), inc=sV)
        S.wait("vector", sP0, max(lds))
        self.identb_done = S.op("vector", lambda e: e.tensor_copy(S.identb, ident),
                                inc=sV)
        S.gset(("wreg",), S.sPM, self.pm_done)

    def p2_mod(self):
        S = self
        PB, ident = S.PB, S.ident
        sMM, sV, sA_ = S.sMM, S.sV, S.sA
        wmb = [S.vf(XG, FC, 512), S.vf(R1AO, FC, 512), S.vf(QF, FC, 512)]
        qeng = ["sync", "gpsimd", "scalar"]
        msrow = S.A[0:1, (SCR + 8192) // 4: (SCR + 8192) // 4 + 512]
        S.wait("tensor", sA_, self.a_silu_c)
        for n in range(24):
            bi = n % 3
            eng = qeng[bi]
            S.gwait(eng, ("wmb", bi))
            src = S.d_wmod[n].rearrange("(a q) w -> q a w", q=128)
            lds2 = [S.dma(eng, wmb[bi][:, fc, :], src[:, fc, :], S.sWM[bi][fc // 4])
                    for fc in range(FC)]
            S.gwait("tensor", ("pb", 4))
            for fc in range(FC):
                if fc % 4 == 0:
                    S.wait("tensor", S.sWM[bi][fc // 4], lds2[fc + 3])
                S.op("tensor", lambda e, bi=bi, fc=fc: e.matmul(
                    PB[4][0:1, :], S.sct[:, fc:fc + 1], wmb[bi][:, fc, :],
                    start=(fc == 0), stop=(fc == 15)),
                     inc=(sMM if fc == 15 else None))
            mmn = sMM.n
            S.gset(("wmb", bi), sMM, mmn)
            S.wait("vector", sMM, mmn)
            evm = S.op("vector", lambda e: e.tensor_copy(msrow, PB[4][0:1, :]), inc=sV)
            S.gset(("pb", 4), sV, evm)
            S.wait("tensor", sV, evm)
            S.gwait("tensor", ("pb", 5))
            for jj in range(4):
                S.op("tensor", lambda e, jj=jj: e.matmul(
                    PB[5][:, jj:jj + 1], msrow[:, jj * 128:(jj + 1) * 128],
                    ident[0:1, 0:1], is_transpose=True,
                    start=(jj == 0), stop=(jj == 3)),
                     inc=(sMM if jj == 3 else None))
            mmt = sMM.n
            S.wait("vector", sMM, mmt)
            vec, fc0 = n // 4, 4 * (n % 4)
            dst = S.modv[:, vec * 16 + fc0: vec * 16 + fc0 + 4]
            evq = S.op("vector", lambda e, d=dst: e.tensor_copy(d, PB[5][:, 0:4]),
                       inc=sV)
            S.gset(("pb", 5), sV, evq)
        S.wait("vector", S.sPM, self.pm_done)
        mv, mv1 = S.mv, S.mv1

        def vop(fn, inc=None):
            return S.op("vector", fn, inc=inc)

        vop(lambda e: e.tensor_scalar_add(mv(MV_CA1), mv(MV_CA1), 1.0))
        vop(lambda e: e.tensor_scalar_add(mv(MV_CB1), mv(MV_CB1), 1.0))
        for c in (MV_GA, MV_GB):
            g = mv(c)
            tpos, tneg, tmask = mv(MV_T0), mv(MV_T0 + 16), mv(MV_T0 + 32)
            vop(lambda e, g=g, t=tpos: e.tensor_scalar_max(t, g, 0.25))
            vop(lambda e, g=g, t=tneg: e.tensor_scalar_min(t, g, -0.25))
            vop(lambda e, g=g, t=tmask: e.tensor_single_scalar(t, g, 0.0, ALU.is_ge))
            vop(lambda e, a=tpos, b=tneg: e.tensor_sub(a, a, b))
            vop(lambda e, m=tmask, a=tpos: e.tensor_mul(a, a, m))
            vop(lambda e, g=g, a=tpos, b=tneg: e.tensor_add(g, a, b))
        vop(lambda e: e.tensor_mul(mv(MV_W2), mv(MV_CA1), mv(MV_CA1)))
        vop(lambda e: e.tensor_mul(mv(MV_SFWN), mv(MV_CA1), mv(MV_WN)))
        vop(lambda e: e.tensor_mul(mv(MV_WSAB), mv(MV_CA1), mv(MV_SA)))
        vop(lambda e: e.reciprocal(mv(MV_T0), mv(MV_CA1)))
        vop(lambda e: e.tensor_mul(mv(MV_SACA), mv(MV_SA), mv(MV_T0)))
        vop(lambda e: e.tensor_mul(mv(MV_SDGB), S.sgsu[:, 88:104], mv(MV_GB)))
        vop(lambda e: e.tensor_mul(mv(MV_T0), mv(MV_SA), mv(MV_SA)))
        vC = vop(lambda e: e.reduce_sum(mv1(MV_T0 + 16), mv(MV_T0), axis=AX.X),
                 inc=sV)
        S.wait("tensor", sV, vC)
        S.gwait("tensor", ("pb", 4))
        mC = S.op("tensor", lambda e: e.matmul(PB[4][0:1, 0:1], mv1(MV_ONE),
                                               mv1(MV_T0 + 16), start=True,
                                               stop=True), inc=sMM)
        S.wait("vector", sMM, mC)
        vC2 = S.op("vector", lambda e: e.tensor_copy(msrow[:, 0:1], PB[4][0:1, 0:1]),
                   inc=sV)
        vOr = S.op("vector", lambda e: e.memset(msrow[:, 1:129], 1.0), inc=sV)
        S.wait("tensor", sV, vOr)
        mC2 = S.op("tensor", lambda e: e.matmul(PB[4][:, 1:2], msrow[:, 1:129],
                                                msrow[:, 0:1], start=True,
                                                stop=True), inc=sMM)
        S.wait("vector", sMM, mC2)
        vC3 = S.op("vector", lambda e: e.tensor_copy(mv1(MV_C), PB[4][:, 1:2]),
                   inc=sV)
        S.gset(("pb", 4), sV, vC3)
        S.gwait("tensor", ("pb", 5))
        mS = S.op("tensor", lambda e: e.matmul(PB[5][0:16, 0:128], mv(MV_SACA),
                                               ident, is_transpose=True,
                                               start=True, stop=True), inc=sMM)
        t16 = S.A[0:16, (SCR + 12288) // 4: (SCR + 12288) // 4 + 128]
        S.wait("vector", sMM, mS)
        vS = S.op("vector", lambda e: e.tensor_copy(t16, PB[5][0:16, 0:128]), inc=sV)
        S.gset(("pb", 5), sV, vS)
        S.wait("gpsimd", S.sV, vS)
        self.saca_done = S.dma("gpsimd",
                               self.saca_row.rearrange("p (a b) -> p a b", b=128),
                               t16, S.sCS)
        S.gset(("scr",), S.sCS, self.saca_done)
        self.mod_done = vC3

    def b1_ln1(self, g):
        S = self
        PB, ident = S.PB, S.ident
        sMM, sV, sA_ = S.sMM, S.sV, S.sA
        xcs = [S.xc, S.vf(KF, D)]

        def stage_a(i):
            gi = 4 * g + i
            if g == 0 and i == 0:
                S.gwait("sync", ("wmb", 0))
            S.gwait("sync", ("xg", i))
            vx = S.dma("sync", S.xg[:, i, :],
                       S.d_x[gi * 128:(gi + 1) * 128, :], S.sX[i])
            xt = S.xg[:, i, :]
            if i == 0:
                S.wreg_barrier("scalar")
                if g == 0:
                    S.wait("tensor", S.sV, self.mod_done)
                    S.wait("scalar", S.sV, self.v_consts)
            S.gwait("scalar", ("wreg",))
            S.wait("scalar", S.sX[i], vx)
            aq = S.op("scalar", lambda e, xt=xt, gi=gi: e.activation(
                out=S.sqdump, in_=xt, func=AF.Square,
                accum_out=S.stv(gi, ST_T0)), inc=sA_)
            S.gset(("wreg",), sA_, aq)
            S.wait("vector", S.sX[i], vx)
            S.op("vector", lambda e, xt=xt, gi=gi: e.reduce_sum(
                S.stv(gi, ST_T1), xt, axis=AX.X))
            S.op("vector", lambda e, gi=gi: e.tensor_scalar_mul(
                S.stv(gi, ST_MU), S.stv(gi, ST_T1), INV_D))
            S.wait("vector", sA_, aq)
            S.op("vector", lambda e, gi=gi: e.tensor_scalar_mul(
                S.stv(gi, ST_T2), S.stv(gi, ST_T0), INV_D))
            S.op("vector", lambda e, gi=gi: e.tensor_mul(
                S.stv(gi, ST_T3), S.stv(gi, ST_MU), S.stv(gi, ST_MU)))
            vv = S.op("vector", lambda e, gi=gi: e.tensor_sub(
                S.stv(gi, ST_T2), S.stv(gi, ST_T2), S.stv(gi, ST_T3)), inc=sV)
            S.wait("scalar", sV, vv)
            asq = S.op("scalar", lambda e, gi=gi: e.activation(
                out=S.stv(gi, ST_T3), in_=S.stv(gi, ST_T2), func=AF.Sqrt,
                bias=S.mv1(MV_EPSLN), scale=1.0), inc=sA_)
            S.wait("vector", sA_, asq)
            S.op("vector", lambda e, gi=gi: e.reciprocal(
                S.stv(gi, ST_RSTD), S.stv(gi, ST_T3)))
            xc = xcs[i % 2]
            if i % 2 == 0:
                S.gwait("vector", ("scr",))
            else:
                S.gwait("vector", ("zfree",))
            S.gwait("vector", ("xcs", i % 2))
            vxc = S.op("vector", lambda e, xt=xt, gi=gi, xc=xc: e.tensor_scalar(
                xc[:, :], xt, S.stv(gi, ST_MU), None, ALU.subtract), inc=sV)
            if i == 3:
                S.gset(("xgf_v",), sV, vxc)
                S.gset(("xgf_a",), sA_, sA_.n)
            return vxc

        def stage_b(i, vxc):
            gi = 4 * g + i
            xc = xcs[i % 2]
            evq = {}
            for q in range(4):
                bk = q % 2
                S.gwait("tensor", ("pb", bk))
                if q == 0:
                    S.wait("tensor", sV, vxc)
                for jj in range(4):
                    fc = 4 * q + jj
                    S.op("tensor", lambda e, bk=bk, jj=jj, fc=fc, xc=xc: e.matmul(
                        PB[bk][:, jj * 128:(jj + 1) * 128],
                        xc[:, fc * 128:(fc + 1) * 128], ident,
                        is_transpose=True, start=(jj == 0), stop=(jj == 3)),
                         inc=(sMM if jj == 3 else None))
                mq = sMM.n
                S.wait("vector", sMM, mq)
                if g == 0 and i == 0 and q == 0:
                    S.wait("vector", S.sCS, self.saca_done)
                evq[q] = S.op("vector", lambda e, bk=bk, q=q: e.tensor_copy(
                    S.xcf[:, 4 * q:4 * q + 4, :],
                    PB[bk].rearrange("p (a b) -> p a b", b=128)), inc=sV)
                S.gset(("pb", bk), sV, evq[q])
            S.gwait("scalar", ("wreg",))
            S.wait("scalar", sV, evq[3])
            asq2 = S.op("scalar", lambda e: e.activation(
                out=S.xcsq.rearrange("p a b -> p (a b)"),
                in_=S.xcf.rearrange("p a b -> p (a b)"), func=AF.Square), inc=sA_)
            S.gwait("tensor", ("pb", 2))
            S.gwait("tensor", ("pb", 3))
            S.wait("tensor", sA_, asq2)
            for fc in range(FC):
                S.op("tensor", lambda e, fc=fc: e.matmul(
                    PB[2][0:1, 0:128], S.mv(MV_W2)[:, fc:fc + 1],
                    S.xcsq[:, fc, :], start=(fc == 0), stop=(fc == 15)))
            for fc in range(FC):
                S.op("tensor", lambda e, fc=fc: e.matmul(
                    PB[3][0:1, 0:128], S.mv(MV_WSAB)[:, fc:fc + 1],
                    S.xcf[:, fc, :], start=(fc == 0), stop=(fc == 15)),
                     inc=(sMM if fc == 15 else None))
            mAB = sMM.n
            S.gset(("wreg",), sMM, mAB)
            S.wait("vector", sMM, mAB)
            S.gwait("vector", ("absb",))
            S.op("vector", lambda e: e.tensor_copy(S.row_a, PB[2][0:1, 0:128]))
            vab = S.op("vector", lambda e: e.tensor_copy(S.row_b, PB[3][0:1, 0:128]),
                       inc=sV)
            S.gset(("pb", 2), sV, vab)
            S.wait("tensor", sV, vab)
            S.op("tensor", lambda e: e.matmul(PB[3][:, 0:1], S.row_a,
                                              ident[0:1, 0:1], is_transpose=True,
                                              start=True, stop=True))
            mABt = S.op("tensor", lambda e: e.matmul(
                PB[3][:, 1:2], S.row_b, ident[0:1, 0:1], is_transpose=True,
                start=True, stop=True), inc=sMM)
            S.gset(("absb",), sMM, mABt)
            S.wait("vector", sMM, mABt)
            vabt = S.op("vector", lambda e, gi=gi: e.tensor_copy(
                S.stv(gi, ST_T0, 2), PB[3][:, 0:2]), inc=sV)
            S.gset(("pb", 3), sV, vabt)
            S.op("vector", lambda e, gi=gi: e.tensor_mul(
                S.stv(gi, ST_T2), S.stv(gi, ST_RSTD), S.stv(gi, ST_RSTD)))
            S.op("vector", lambda e, gi=gi: e.tensor_mul(
                S.stv(gi, ST_T2), S.stv(gi, ST_T2), S.stv(gi, ST_T0)))
            S.op("vector", lambda e, gi=gi: e.tensor_mul(
                S.stv(gi, ST_T3), S.stv(gi, ST_RSTD), S.stv(gi, ST_T1)))
            S.op("vector", lambda e, gi=gi: e.tensor_scalar_mul(
                S.stv(gi, ST_T3), S.stv(gi, ST_T3), 2.0))
            S.op("vector", lambda e, gi=gi: e.tensor_add(
                S.stv(gi, ST_T2), S.stv(gi, ST_T2), S.stv(gi, ST_T3)))
            S.op("vector", lambda e, gi=gi: e.tensor_add(
                S.stv(gi, ST_T2), S.stv(gi, ST_T2), S.mv1(MV_C)))
            vrr = S.op("vector", lambda e, gi=gi: e.tensor_scalar_mul(
                S.stv(gi, ST_T2), S.stv(gi, ST_T2), INV_D), inc=sV)
            S.wait("scalar", sV, vrr)
            ar = S.op("scalar", lambda e, gi=gi: e.activation(
                out=S.stv(gi, ST_T3), in_=S.stv(gi, ST_T2), func=AF.Sqrt,
                bias=S.mv1(MV_EPSRMS), scale=1.0), inc=sA_)
            S.wait("vector", sA_, ar)
            S.op("vector", lambda e, gi=gi: e.reciprocal(
                S.stv(gi, ST_RRMS), S.stv(gi, ST_T3)))
            vsc = S.op("vector", lambda e, gi=gi: e.tensor_mul(
                S.stv(gi, ST_SC1), S.stv(gi, ST_RSTD), S.stv(gi, ST_RRMS)), inc=sV)
            S.wait("tensor", sV, vsc)
            S.gwait("tensor", ("pb", 3))
            mrr = S.op("tensor", lambda e, gi=gi: e.matmul(
                PB[3][0:1, 0:128], S.stv(gi, ST_RRMS), ident,
                is_transpose=True, start=True, stop=True), inc=sMM)
            S.wait("vector", sMM, mrr)
            S.gwait("vector", ("rrow",))
            vrow = S.op("vector", lambda e: e.tensor_copy(
                S.rrow_bf, PB[3][0:1, 0:128]), inc=sV)
            S.gset(("pb", 3), sV, vrow)
            S.gwait("vector", ("d1t",))
            vd1 = S.op("vector", lambda e, gi=gi: e.tensor_scalar_mul(
                S.d1t, ident, S.stv(gi, ST_SC1)), inc=sV)
            if g == 0 and i == 0:
                S.wait("tensor", S.sCS, self.saca_done)
            for q in range(4):
                bk = q % 2
                S.gwait("tensor", ("pb", bk))
                if q == 0:
                    S.wait("tensor", sV, max(vd1, vrow))
                for jj in range(4):
                    fc = 4 * q + jj
                    S.op("tensor", lambda e, bk=bk, jj=jj, fc=fc, xc=xc: e.matmul(
                        PB[bk][:, jj * 128:(jj + 1) * 128],
                        xc[:, fc * 128:(fc + 1) * 128], S.d1t,
                        start=(jj == 0), stop=False))
                    S.op("tensor", lambda e, bk=bk, jj=jj, fc=fc: e.matmul(
                        PB[bk][:, jj * 128:(jj + 1) * 128],
                        S.saca_row[:, fc * 128:(fc + 1) * 128], S.rrow_bf,
                        start=False, stop=(jj == 3)),
                         inc=(sMM if jj == 3 else None))
                mq = sMM.n
                if q == 3:
                    S.gset(("xcs", i % 2), sMM, mq)
                    if i % 2 == 0:
                        S.gset(("scr",), sMM, mq)
                    else:
                        S.gset(("xc2",), sMM, mq)
                    S.gset(("d1t",), sMM, mq)
                    S.gset(("rrow",), sMM, mq)
                S.wait("vector", sMM, mq)
                if i == 0 and q == 0:
                    S.gwait("vector", ("gufree",))
                    if g == 0:
                        S.gwait("vector", ("wmb", 1))
                sf = S.mv(MV_SFWN)[:, 4 * q:4 * q + 4].unsqueeze(2) \
                    .broadcast_to((128, 4, 128))
                tmpf = S.xcf.rearrange("p a b -> p (a b)")[:, 0:512] \
                    .rearrange("p (a b) -> p a b", b=128)
                S.op("vector", lambda e, bk=bk, sf=sf, t=tmpf: e.tensor_mul(
                    t, PB[bk].rearrange("p (a b) -> p a b", b=128), sf))
                S.op("vector", lambda e, q=q, i=i, t=tmpf: e.tensor_copy(
                    S.r1hi[:, 4 * q:4 * q + 4, i * 128:(i + 1) * 128], t))
                evr = S.op("vector", lambda e, q=q, i=i, t=tmpf: e.tensor_tensor(
                    S.r1lo[:, 4 * q:4 * q + 4, i * 128:(i + 1) * 128], t,
                    S.r1hi[:, 4 * q:4 * q + 4, i * 128:(i + 1) * 128],
                    ALU.subtract), inc=sV)
                S.gset(("pb", bk), sV, evr)

        va = {0: stage_a(0), 1: stage_a(1)}
        stage_b(0, va[0])
        va[2] = stage_a(2)
        stage_b(1, va[1])
        va[3] = stage_a(3)
        stage_b(2, va[2])
        stage_b(3, va[3])
        self.r1_done = sV.n

    def b2_qkv(self, g):
        S = self
        PB = S.PB
        sMM, sV, sA_ = S.sMM, S.sV, S.sA
        S.wreg_barrier("sync")
        S.wait("tensor", sV, self.r1_done)

        def emit_qk_mms(bk, hiq, loq, j, fc, first, last, ready, qend):
            if ready is not None:
                S.wait("tensor", *ready)
            S.op("tensor", lambda e: e.matmul(
                PB[bk], hiq[:, j, :], S.r1hi[:, fc, :],
                start=first, stop=False))
            S.op("tensor", lambda e: e.matmul(
                PB[bk], hiq[:, j, :], S.r1lo[:, fc, :],
                start=False, stop=False))
            S.op("tensor", lambda e: e.matmul(
                PB[bk], loq[:, j, :], S.r1hi[:, fc, :],
                start=False, stop=last), inc=(sMM if qend else None))

        def stream_oc(oc, bk, d_f32, d_hi, d_lo, nsub):
            if g == 0:
                for qt in range(4):
                    slot = (self._wtc + qt) % 3
                    off = WREG + slot * 4096
                    f32q = S.vf(off, 4, 128)
                    hiq = S.vb(off + 2048, 4, 128)
                    loq = S.vb(off + 3072, 4, 128)
                    S.gwait("sync", ("wt", slot))
                    S.gwait("sync", ("wts", slot))
                    src = d_f32[oc].rearrange("(a q) w -> q a w", q=128)[:, qt * 4:qt * 4 + 4, :]
                    ld = S.dma("sync", f32q, src, S.sWT[slot])
                    S.gwait("vector", ("wt", slot))
                    S.wait("vector", S.sWT[slot], ld)
                    S.op("vector", lambda e, f=f32q, h=hiq: e.tensor_copy(h, f))
                    vsp = S.op("vector", lambda e, f=f32q, h=hiq, l=loq:
                               e.tensor_tensor(l, f, h, ALU.subtract), inc=sV)
                    S.wait("gpsimd", sV, vsp)
                    dh = d_hi[oc].rearrange("(a q) w -> q a w", q=128)[:, qt * 4:qt * 4 + 4, :]
                    dl = d_lo[oc].rearrange("(a q) w -> q a w", q=128)[:, qt * 4:qt * 4 + 4, :]
                    S.dma("gpsimd", dh, hiq, S.sQS[slot])
                    sq = S.dma("gpsimd", dl, loq, S.sQS[slot])
                    S.gset(("wts", slot), S.sQS[slot], sq)
                    for j in range(4):
                        fc = qt * 4 + j
                        emit_qk_mms(bk, hiq, loq, j, fc, fc == 0, fc == 15,
                                    (sV, vsp) if j == 0 else None, j == 3)
                    S.gset(("wt", slot), sMM, sMM.n)
                self._wtc += 4
            else:
                for hf in range(2):
                    slot = (self._wtc + hf) % 3
                    off = WREG + slot * 4096
                    hiq = S.vb(off, 8, 128)
                    loq = S.vb(off + 2048, 8, 128)
                    S.gwait("sync", ("wt", slot))
                    sh = d_hi[oc].rearrange("(a q) w -> q a w", q=128)[:, hf * 8:hf * 8 + 8, :]
                    sl = d_lo[oc].rearrange("(a q) w -> q a w", q=128)[:, hf * 8:hf * 8 + 8, :]
                    S.dma("sync", hiq, sh, S.sWT[slot])
                    ld = S.dma("sync", loq, sl, S.sWT[slot])
                    for j in range(8):
                        fc = hf * 8 + j
                        emit_qk_mms(bk, hiq, loq, j, fc, fc == 0, fc == 15,
                                    (S.sWT[slot], ld) if j == 0 else None, j == 7)
                    S.gset(("wt", slot), sMM, sMM.n)
                self._wtc += 2
            return sMM.n

        self._wtc = 0
        for oc in range(32):
            bk = oc % 3
            S.gwait("tensor", ("pb", bk))
            m = stream_oc(oc, bk, S.d_wqkv, S.d_wqh, S.d_wql, 4)
            if oc < 16:
                S.wait("vector", sMM, m)
                if oc == 0:
                    S.gwait("vector", ("gufree",))
                    S.gwait("vector", ("wmb", 2))
                ev = S.op("vector", lambda e, bk=bk, oc=oc: e.tensor_scalar_mul(
                    S.qf[:, oc, :], PB[bk], SCALE), inc=sV)
                S.gset(("pb", bk), sV, ev)
            else:
                S.wait("scalar", sMM, m)
                if oc == 16:
                    S.gwait("scalar", ("zfree",))
                    S.gwait("scalar", ("xc2",))
                ev = S.op("scalar", lambda e, bk=bk, oc=oc: e.activation(
                    out=S.kf[:, oc - 16, :], in_=PB[bk], func=AF.Copy), inc=sA_)
                S.gset(("pb", bk), sA_, ev)

        self.qkv_done_v = sV.n
        self.qkv_done_a = sA_.n

    def b3_attn(self, g):
        S = self
        PB, ident = S.PB, S.ident
        sMM, sV, sA_ = S.sMM, S.sV, S.sA
        S.gwait("vector", ("scr",))
        if g == 0:
            S.wait("tensor", S.sPM, self.pm_done)
        S.wait("tensor", S.sV, self.qkv_done_v)
        S.wait("tensor", S.sA, self.qkv_done_a)
        S.wreg_barrier("sync")

        vstep = {"n": 0}
        os_mark = {}

        def emit_v_step():
            n = vstep["n"]
            if n >= 128:
                return
            vstep["n"] = n + 1
            os_, pair, fc = n // 32, (n // 16) % 2, n % 16
            slot = n % 3
            off = WREG + slot * 4096
            hiq = S.vb(off + 2048, 4, 128)
            loq = S.vb(off + 3072, 4, 128)
            hiq_f = hiq.rearrange("p a b -> p (a b)")
            loq_f = loq.rearrange("p a b -> p (a b)")
            oc0 = 32 + 4 * os_
            if g == 0:
                f32q = S.vf(off, 4, 128)
                S.gwait("sync", ("vt", slot))
                S.gwait("sync", ("vts", slot))
                srcw = S.d_wqkv[oc0:oc0 + 4, fc * 128:(fc + 1) * 128, :] \
                    .transpose([1, 0, 2])
                ld = S.dma("sync", f32q, srcw, S.sVT[slot])
                S.gwait("vector", ("vt", slot))
                S.wait("vector", S.sVT[slot], ld)
                S.op("vector", lambda e, f=f32q, h=hiq: e.tensor_copy(h, f))
                vsp = S.op("vector", lambda e, f=f32q, h=hiq, l=loq:
                           e.tensor_tensor(l, f, h, ALU.subtract), inc=sV)
                if pair == 0:
                    S.wait("gpsimd", sV, vsp)
                    dh = S.d_wqh[oc0:oc0 + 4, fc * 128:(fc + 1) * 128, :] \
                        .transpose([1, 0, 2])
                    dl = S.d_wql[oc0:oc0 + 4, fc * 128:(fc + 1) * 128, :] \
                        .transpose([1, 0, 2])
                    S.dma("gpsimd", dh, hiq, S.sVS[slot])
                    sq = S.dma("gpsimd", dl, loq, S.sVS[slot])
                    S.gset(("vts", slot), S.sVS[slot], sq)
                ready = (sV, vsp)
            else:
                S.gwait("sync", ("vt", slot))
                sh = S.d_wqh[oc0:oc0 + 4, fc * 128:(fc + 1) * 128, :] \
                    .transpose([1, 0, 2])
                sl = S.d_wql[oc0:oc0 + 4, fc * 128:(fc + 1) * 128, :] \
                    .transpose([1, 0, 2])
                S.dma("sync", hiq, sh, S.sVT[slot])
                ld = S.dma("sync", loq, sl, S.sVT[slot])
                ready = (S.sVT[slot], ld)
            S.wait("tensor", *ready)
            for w in range(2):
                tt = 2 * pair + w
                bk = 6 + w
                if fc == 0:
                    S.gwait("tensor", ("pb", bk))
                S.op("tensor", lambda e, bk=bk, tt=tt, fc=fc, h=hiq_f: e.matmul(
                    PB[bk], S.r1hi[:, fc, tt * 128:(tt + 1) * 128], h,
                    start=(fc == 0), stop=False))
                S.op("tensor", lambda e, bk=bk, tt=tt, fc=fc, h=hiq_f: e.matmul(
                    PB[bk], S.r1lo[:, fc, tt * 128:(tt + 1) * 128], h,
                    start=False, stop=False))
                S.op("tensor", lambda e, bk=bk, tt=tt, fc=fc, l=loq_f: e.matmul(
                    PB[bk], S.r1hi[:, fc, tt * 128:(tt + 1) * 128], l,
                    start=False, stop=(fc == 15)),
                     inc=(sMM if w == 1 else None))
            S.gset(("vt", slot), sMM, sMM.n)
            if fc == 15:
                mv_ = sMM.n
                for w in range(2):
                    tt = 2 * pair + w
                    bk = 6 + w
                    eng, sem = (("vector", sV) if w == 0 else ("scalar", sA_))
                    S.wait(eng, sMM, mv_)
                    if os_ == 0 and pair == 0:
                        S.gwait(eng, ("gufree",))
                    if eng == "vector":
                        ev = S.op(eng, lambda e, bk=bk, tt=tt, os_=os_:
                                  e.tensor_copy(S.vtm[:, tt, os_ * 512:(os_ + 1) * 512],
                                                PB[bk]), inc=sem)
                    else:
                        ev = S.op(eng, lambda e, bk=bk, tt=tt, os_=os_:
                                  e.activation(out=S.vtm[:, tt, os_ * 512:(os_ + 1) * 512],
                                               in_=PB[bk], func=AF.Copy), inc=sem)
                    S.gset(("pb", bk), sem, ev)
                if pair == 1:
                    os_mark[os_] = (sV.n, sA_.n)
            if vstep["n"] == 128:
                S.gset(("r1free",), sMM, sMM.n)

        st = {}

        def emit_S(k):
            hc, par = k // 2, k % 2
            sb = k % 2
            sc = k % 3
            mask = S.maske if par == 0 else S.masko
            S.gwait("tensor", ("pb", sb))
            if g == 0 and k == 0:
                S.wait("tensor", sV, self.identb_done)
            for r in range(2):
                rh = r * 64
                S.op("tensor", lambda e, sb=sb, rh=rh, mask=mask: e.matmul(
                    PB[sb][rh:rh + 64, :], S.identb[rh:rh + 64, rh:rh + 64],
                    mask[rh:rh + 64, :], start=True, stop=False))
                for s in range(4):
                    w = 2 * s + par
                    Q = S.qf[rh:rh + 64, hc, w * 64:(w + 1) * 64]
                    K = S.kf[rh:rh + 64, hc, w * 64:(w + 1) * 64]
                    pk = S.pmk[rh:rh + 64, hc, :]
                    if par == 0:
                        c_tok, c_pm = s * 128, s * 128 + 64
                    else:
                        c_tok, c_pm = s * 128 + 64, s * 128
                    S.op("tensor", lambda e, sb=sb, rh=rh, c=c_tok, Q=Q, K=K:
                         e.matmul(PB[sb][rh:rh + 64, c:c + 64], Q, K,
                                  start=False, stop=False))
                    S.op("tensor", lambda e, sb=sb, rh=rh, c=c_pm, Q=Q, pk=pk,
                         l=(s == 3): e.matmul(PB[sb][rh:rh + 64, c:c + 16], Q, pk,
                                              start=False, stop=l),
                         inc=(sMM if (s == 3 and r == 1) else None))
            st[k] = {"mS": sMM.n, "sb": sb, "sc": sc}

        def emit_softmax(k):
            hc, par = k // 2, k % 2
            d = st[k]
            sb, sc = d["sb"], d["sc"]
            ns = 4 * (k % 3)
            S.gwait("scalar", ("smb", sc))
            S.wait("scalar", sMM, d["mS"])
            aE = None
            for s in range(4):
                aE = S.op("scalar", lambda e, sb=sb, sc=sc, s=s, ns=ns:
                          e.activation(out=S.pnb[sc][:, s, :],
                                       in_=PB[sb][:, s * 128:(s + 1) * 128],
                                       func=AF.Exp,
                                       accum_out=S.negm[:, ns + s:ns + s + 1]),
                          inc=(sA_ if s == 3 else None))
            S.gset(("pb", sb), sA_, aE)
            S.wait("vector", sA_, aE)
            S.op("vector", lambda e, ns=ns: e.reciprocal(
                S.negm[:, ns:ns + 4], S.negm[:, ns:ns + 4]))
            vPn = S.op("vector", lambda e, sc=sc, ns=ns: e.tensor_tensor(
                S.pnb[sc], S.pnb[sc],
                S.negm[:, ns:ns + 4].unsqueeze(2).broadcast_to((128, 4, 128)),
                ALU.mult), inc=sV)
            d["vPn"] = vPn

        def emit_ptav(k):
            hc, par = k // 2, k % 2
            d = st[k]
            sc = d["sc"]
            psc = k % 2
            ptb = 2 + k % 2
            S.gwait("tensor", ("pb", ptb))
            S.wait("tensor", sV, d["vPn"])
            for s in range(4):
                S.op("tensor", lambda e, ptb=ptb, sc=sc, s=s:
                     e.matmul(PB[ptb][:, s * 128:(s + 1) * 128],
                              S.pnb[sc][:, s, :], ident,
                              is_transpose=True, start=True, stop=True),
                     inc=(sMM if s == 3 else None))
            mT = sMM.n
            S.gset(("smb", sc), sMM, mT)
            S.gwait("vector", ("pts", psc))
            S.wait("vector", sMM, mT)
            vPT = S.op("vector", lambda e, ptb=ptb, sc=psc: e.tensor_copy(
                S.ptsb[sc].rearrange("p a b -> p (a b)"), PB[ptb]), inc=sV)
            S.gset(("pb", ptb), sV, vPT)
            need = hc // 4
            mk = os_mark[need]
            S.wait("tensor", sV, max(vPT, mk[0]))
            S.wait("tensor", sA_, mk[1])
            if par == 0:
                S.gwait("tensor", ("pb", 4))
                S.gwait("tensor", ("pb", 5))
                self._ao_started = [False, False]
            for s in range(4):
                for r in range(2):
                    h = 2 * hc + r
                    w = 2 * s + par
                    cq = slice(r * 64, r * 64 + 64)
                    ab = 4 if r == 0 else 5
                    if par == 0:
                        tokL = S.vtm[0:64, w // 2, h * 64:(h + 1) * 64]
                        tokR = S.ptsb[psc][0:64, s, cq]
                        pmL = S.pmv[64:80, h, :]
                        pmR = S.ptsb[psc][64:80, s, cq]
                    else:
                        tokL = S.vtm[64:128, w // 2, h * 64:(h + 1) * 64]
                        tokR = S.ptsb[psc][64:128, s, cq]
                        pmL = S.pmv[0:16, h, :]
                        pmR = S.ptsb[psc][0:16, s, cq]
                    out = PB[ab][r * 64:r * 64 + 64, w * 64:(w + 1) * 64]
                    startf = not self._ao_started[r]
                    self._ao_started[r] = True
                    stopf = (par == 1 and s == 3)
                    S.op("tensor", lambda e, o=out, L=tokL, R=tokR, f=startf:
                         e.matmul(o, L, R, start=f, stop=False))
                    S.op("tensor", lambda e, o=out, L=pmL, R=pmR, stp=stopf:
                         e.matmul(o, L, R, start=False, stop=stp),
                         inc=(sMM if (s == 3 and r == 1) else None))
            mAV = sMM.n
            S.gset(("pts", psc), sMM, mAV)
            if par == 1:
                S.wait("vector", sMM, mAV)
                if hc == 0:
                    S.gwait("vector", ("xgf_v",))
                    S.gwait("vector", ("xgf_a",))
                S.op("vector", lambda e, hc=hc: e.tensor_copy(
                    S.aohi[0:64, hc, :], PB[4][0:64, :]))
                S.op("vector", lambda e, hc=hc: e.tensor_tensor(
                    S.aolo[0:64, hc, :], PB[4][0:64, :],
                    S.aohi[0:64, hc, :], ALU.subtract))
                S.op("vector", lambda e, hc=hc: e.tensor_copy(
                    S.aohi[64:128, hc, :], PB[5][64:128, :]))
                evAO = S.op("vector", lambda e, hc=hc: e.tensor_tensor(
                    S.aolo[64:128, hc, :], PB[5][64:128, :],
                    S.aohi[64:128, hc, :], ALU.subtract), inc=sV)
                S.gset(("pb", 4), sV, evAO)
                S.gset(("pb", 5), sV, evAO)

        for _ in range(32):
            emit_v_step()
        last_smm = None
        for k in range(34):
            if k < 32:
                if k < 24:
                    for _ in range(4):
                        emit_v_step()
                emit_S(k)
                last_smm = st[k]["mS"]
                emit_softmax(k)
            if k >= 2:
                emit_ptav(k - 2)
        S.gset(("kf",), sMM, last_smm)
        S.gset(("scr",), sMM, sMM.n)
        self.ao_done = self.sV.n

    def b4_outproj(self, g):
        S = self
        PB = S.PB
        sMM, sV = S.sMM, S.sV
        S.wreg_barrier("sync")
        S.wait("tensor", sV, self.ao_done)
        self._wtc = 0

        def emit_mms(bk, hiq, loq, j, fc, first, last, ready, qend):
            if ready is not None:
                S.wait("tensor", *ready)
            S.op("tensor", lambda e: e.matmul(
                PB[bk], hiq[:, j, :], S.aohi[:, fc, :], start=first, stop=False))
            S.op("tensor", lambda e: e.matmul(
                PB[bk], hiq[:, j, :], S.aolo[:, fc, :], start=False, stop=False))
            S.op("tensor", lambda e: e.matmul(
                PB[bk], loq[:, j, :], S.aohi[:, fc, :], start=False, stop=last),
                 inc=(sMM if qend else None))

        for oc in range(16):
            bk = oc % 2
            S.gwait("tensor", ("pb", bk))
            if g == 0:
                for qt in range(4):
                    slot = (self._wtc + qt) % 3
                    off = WREG + slot * 4096
                    f32q = S.vf(off, 4, 128)
                    hiq = S.vb(off + 2048, 4, 128)
                    loq = S.vb(off + 3072, 4, 128)
                    S.gwait("sync", ("wt", slot))
                    S.gwait("sync", ("wts", slot))
                    src = S.d_wout[oc].rearrange("(a q) w -> q a w", q=128)[:, qt * 4:qt * 4 + 4, :]
                    ld = S.dma("sync", f32q, src, S.sWT[slot])
                    S.gwait("vector", ("wt", slot))
                    S.wait("vector", S.sWT[slot], ld)
                    S.op("vector", lambda e, f=f32q, h=hiq: e.tensor_copy(h, f))
                    vsp = S.op("vector", lambda e, f=f32q, h=hiq, l=loq:
                               e.tensor_tensor(l, f, h, ALU.subtract), inc=sV)
                    S.wait("gpsimd", sV, vsp)
                    dh = S.d_woh[oc].rearrange("(a q) w -> q a w", q=128)[:, qt * 4:qt * 4 + 4, :]
                    dl = S.d_wol[oc].rearrange("(a q) w -> q a w", q=128)[:, qt * 4:qt * 4 + 4, :]
                    S.dma("gpsimd", dh, hiq, S.sQS[slot])
                    sq = S.dma("gpsimd", dl, loq, S.sQS[slot])
                    S.gset(("wts", slot), S.sQS[slot], sq)
                    for j in range(4):
                        fc = qt * 4 + j
                        emit_mms(bk, hiq, loq, j, fc, fc == 0, fc == 15,
                                 (sV, vsp) if j == 0 else None, j == 3)
                    S.gset(("wt", slot), sMM, sMM.n)
                self._wtc += 4
            else:
                for hf in range(2):
                    slot = (self._wtc + hf) % 3
                    off = WREG + slot * 4096
                    hiq = S.vb(off, 8, 128)
                    loq = S.vb(off + 2048, 8, 128)
                    S.gwait("sync", ("wt", slot))
                    sh = S.d_woh[oc].rearrange("(a q) w -> q a w", q=128)[:, hf * 8:hf * 8 + 8, :]
                    sl = S.d_wol[oc].rearrange("(a q) w -> q a w", q=128)[:, hf * 8:hf * 8 + 8, :]
                    S.dma("sync", hiq, sh, S.sWT[slot])
                    ld = S.dma("sync", loq, sl, S.sWT[slot])
                    for j in range(8):
                        fc = hf * 8 + j
                        emit_mms(bk, hiq, loq, j, fc, fc == 0, fc == 15,
                                 (S.sWT[slot], ld) if j == 0 else None, j == 7)
                    S.gset(("wt", slot), sMM, sMM.n)
                self._wtc += 2
            m = sMM.n
            S.wait("vector", sMM, m)
            if oc == 0:
                S.gwait("vector", ("kf",))
            ev = S.op("vector", lambda e, bk=bk, oc=oc: e.tensor_scalar(
                S.opj[:, oc, :], PB[bk], S.mv(MV_GA)[:, oc:oc + 1], None,
                ALU.mult), inc=sV)
            S.gset(("pb", bk), sV, ev)
        self.opj_done = sV.n
        S.gset(("aofree",), sMM, sMM.n)

    def b5_resid(self, g):
        S = self
        PB, ident = S.PB, S.ident
        sMM, sV = S.sMM, S.sV
        xrt = S.vf(XCF, D)
        for tt in range(4):
            gi = 4 * g + tt
            if tt == 0:
                S.gwait("sync", ("scr",))
                S.gwait("sync", ("scr2",))
            S.gwait("sync", ("xrt",))
            vx = S.dma("sync", xrt, S.d_x[gi * 128:(gi + 1) * 128, :], S.sX[tt])
            for q in range(4):
                bk = 2 + (tt * 4 + q) % 2
                S.gwait("tensor", ("pb", bk))
                S.wait("tensor", sV, self.opj_done)
                for jj in range(4):
                    oc = 4 * q + jj
                    S.op("tensor", lambda e, bk=bk, jj=jj, oc=oc, tt=tt: e.matmul(
                        PB[bk][:, jj * 128:(jj + 1) * 128],
                        S.opj[:, oc, tt * 128:(tt + 1) * 128], ident,
                        is_transpose=True, start=(jj == 0), stop=(jj == 3)),
                         inc=(sMM if jj == 3 else None))
                m = sMM.n
                S.wait("vector", sMM, m)
                if q == 0:
                    S.wait("vector", S.sX[tt], vx)
                    if tt == 0:
                        S.gwait("vector", ("aofree",))
                ev = S.op("vector", lambda e, bk=bk, tt=tt, q=q: e.tensor_tensor(
                    S.xg[:, tt, q * 512:(q + 1) * 512], PB[bk],
                    xrt[:, q * 512:(q + 1) * 512], ALU.add), inc=sV)
                S.gset(("pb", bk), sV, ev)
            S.gset(("xrt",), sV, sV.n)
        S.gset(("kf2",), sMM, sMM.n)
        self.x1_done = sV.n

    def b6_ln2(self, g):
        S = self
        PB, ident = S.PB, S.ident
        sMM, sV, sA_ = S.sMM, S.sV, S.sA
        for i in range(4):
            gi = 4 * g + i
            xt = S.xg[:, i, :]
            if i == 0:
                S.wreg_barrier("scalar")
                S.gwait("vector", ("kf2",))
                S.wait("scalar", S.sV, self.x1_done)
            S.gwait("scalar", ("wreg",))
            aq = S.op("scalar", lambda e, xt=xt, gi=gi: e.activation(
                out=S.sqdump, in_=xt, func=AF.Square,
                accum_out=S.stv(gi, ST_T0)), inc=sA_)
            S.gset(("wreg",), sA_, aq)
            S.op("vector", lambda e, xt=xt, gi=gi: e.reduce_sum(
                S.stv(gi, ST_T1), xt, axis=AX.X))
            S.op("vector", lambda e, gi=gi: e.tensor_scalar_mul(
                S.stv(gi, ST_MU), S.stv(gi, ST_T1), INV_D))
            S.wait("vector", sA_, aq)
            S.op("vector", lambda e, gi=gi: e.tensor_scalar_mul(
                S.stv(gi, ST_T2), S.stv(gi, ST_T0), INV_D))
            S.op("vector", lambda e, gi=gi: e.tensor_mul(
                S.stv(gi, ST_T3), S.stv(gi, ST_MU), S.stv(gi, ST_MU)))
            vv = S.op("vector", lambda e, gi=gi: e.tensor_sub(
                S.stv(gi, ST_T2), S.stv(gi, ST_T2), S.stv(gi, ST_T3)), inc=sV)
            S.wait("scalar", sV, vv)
            asq = S.op("scalar", lambda e, gi=gi: e.activation(
                out=S.stv(gi, ST_T3), in_=S.stv(gi, ST_T2), func=AF.Sqrt,
                bias=S.mv1(MV_EPSLN), scale=1.0), inc=sA_)
            S.wait("vector", sA_, asq)
            S.op("vector", lambda e, gi=gi: e.reciprocal(
                S.stv(gi, ST_RSTD), S.stv(gi, ST_T3)))
            S.gwait("vector", ("scr",))
            vxc = S.op("vector", lambda e, xt=xt, gi=gi: e.tensor_scalar(
                S.xc[:, :], xt, S.stv(gi, ST_MU), None, ALU.subtract), inc=sV)
            S.gwait("vector", ("d1t",))
            vd1 = S.op("vector", lambda e, gi=gi: e.tensor_scalar_mul(
                S.d1t, ident, S.stv(gi, ST_RSTD)), inc=sV)
            for q in range(4):
                bk = q % 2
                S.gwait("tensor", ("pb", bk))
                if q == 0:
                    S.wait("tensor", sV, max(vxc, vd1))
                for jj in range(4):
                    fc = 4 * q + jj
                    S.op("tensor", lambda e, bk=bk, jj=jj, fc=fc: e.matmul(
                        PB[bk][:, jj * 128:(jj + 1) * 128],
                        S.xc[:, fc * 128:(fc + 1) * 128], S.d1t,
                        start=(jj == 0), stop=(jj == 3)),
                         inc=(sMM if jj == 3 else None))
                mq = sMM.n
                if q == 3:
                    S.gset(("scr",), sMM, mq)
                    S.gset(("d1t",), sMM, mq)
                S.wait("vector", sMM, mq)
                cb = S.mv(MV_CB1)[:, 4 * q:4 * q + 4].unsqueeze(2) \
                    .broadcast_to((128, 4, 128))
                sb = S.mv(MV_SB)[:, 4 * q:4 * q + 4].unsqueeze(2) \
                    .broadcast_to((128, 4, 128))
                S.gwait("vector", ("tmp512",))
                S.op("vector", lambda e, bk=bk, cb=cb: e.tensor_tensor(
                    S.tmp512, PB[bk].rearrange("p (a b) -> p a b", b=128),
                    cb, ALU.mult))
                vh = S.op("vector", lambda e, sb=sb: e.tensor_tensor(
                    S.tmp512, S.tmp512, sb, ALU.add), inc=sV)
                S.gset(("pb", bk), sV, vh)
                S.op("vector", lambda e, q=q, i=i: e.tensor_copy(
                    S.h2hi[:, 4 * q:4 * q + 4, i * 128:(i + 1) * 128], S.tmp512))
                vlo = S.op("vector", lambda e, q=q, i=i: e.tensor_tensor(
                    S.h2lo[:, 4 * q:4 * q + 4, i * 128:(i + 1) * 128], S.tmp512,
                    S.h2hi[:, 4 * q:4 * q + 4, i * 128:(i + 1) * 128],
                    ALU.subtract), inc=sV)
                S.gset(("tmp512",), sV, vlo)
        self.h2_done = sV.n

    def b7_ffn(self, g):
        S = self
        PB, ident = S.PB, S.ident
        sMM, sV, sA_ = S.sMM, S.sV, S.sA
        gw = [S.vb(WREG + k * 2048, 8, 128) for k in range(3)]
        uw = [S.vb(WREG + 6144 + k * 2048, 8, 128) for k in range(3)]
        S.wreg_barrier("gpsimd")
        S.wait("tensor", S.sV, self.h2_done)
        onesbf = S.vb(SCR + 8448, 1)
        n128row = S.A[0:1, (SCR + 8192) // 4: (SCR + 8192) // 4 + 64].bitcast(BF16)
        csrow = S.A[0:1, (SCR + 4096) // 4: (SCR + 4096) // 4 + 512]
        cshi = S.A[0:1, (SCR + 6144) // 4: (SCR + 6144) // 4 + 256].bitcast(BF16)
        cslo = S.A[0:1, (SCR + 7168) // 4: (SCR + 7168) // 4 + 256].bitcast(BF16)
        S.gwait("vector", ("scr",))
        S.op("vector", lambda e: e.memset(onesbf, 1.0))
        vcn = S.op("vector", lambda e: e.memset(n128row, -128.0), inc=sV)
        S.gset(("scr",), sV, vcn)
        S.gwait("tensor", ("pb", 6))
        S.wait("tensor", sV, vcn)
        for fc in range(FC):
            S.op("tensor", lambda e, fc=fc: e.matmul(
                PB[6][0:1, :], onesbf, S.h2hi[:, fc, :],
                start=(fc == 0), stop=False))
            S.op("tensor", lambda e, fc=fc: e.matmul(
                PB[6][0:1, :], onesbf, S.h2lo[:, fc, :],
                start=False, stop=(fc == 15)), inc=(sMM if fc == 15 else None))
        mcs = sMM.n
        S.wait("vector", sMM, mcs)
        S.op("vector", lambda e: e.tensor_copy(csrow, PB[6][0:1, :]))
        S.op("vector", lambda e: e.tensor_copy(cshi, csrow))
        vcs = S.op("vector", lambda e: e.tensor_tensor(
            cslo, csrow, cshi, ALU.subtract), inc=sV)
        S.gset(("pb", 6), sV, vcs)
        S.wait("tensor", sV, vcs)
        for hid in range(HC):
            lds = {}
            for hf in range(2):
                slot = (hid * 2 + hf) % 3
                S.gwait("gpsimd", ("gw", slot))
                S.gwait("gpsimd", ("uw", slot))
                lds[("g", hf)] = (slot, S.dma(
                    "gpsimd", gw[slot],
                    S.d_wg[hid].rearrange("(a q) w -> q a w", q=128)[:, hf * 8:hf * 8 + 8, :], S.sGW[slot]))
                lds[("u", hf)] = (slot, S.dma(
                    "gpsimd", uw[slot],
                    S.d_wu[hid].rearrange("(a q) w -> q a w", q=128)[:, hf * 8:hf * 8 + 8, :], S.sUW[slot]))
            bg, bu = hid % 2, 2 + hid % 2
            S.gwait("tensor", ("pb", bg))
            S.gwait("tensor", ("pb", bu))
            for fc in range(FC):
                hf = fc // 8
                slot = (hid * 2 + hf) % 3
                if fc % 8 == 0:
                    S.wait("tensor", S.sGW[lds[("g", hf)][0]], lds[("g", hf)][1])
                    S.wait("tensor", S.sUW[lds[("u", hf)][0]], lds[("u", hf)][1])
                S.op("tensor", lambda e, bg=bg, slot=slot, fc=fc: e.matmul(
                    PB[bg], gw[slot][:, fc % 8, :], S.h2hi[:, fc, :],
                    start=(fc == 0), stop=False))
                S.op("tensor", lambda e, bg=bg, slot=slot, fc=fc: e.matmul(
                    PB[bg], gw[slot][:, fc % 8, :], S.h2lo[:, fc, :],
                    start=False, stop=False))
                S.op("tensor", lambda e, bu=bu, slot=slot, fc=fc: e.matmul(
                    PB[bu], uw[slot][:, fc % 8, :], S.h2hi[:, fc, :],
                    start=(fc == 0), stop=False))
                S.op("tensor", lambda e, bu=bu, slot=slot, fc=fc: e.matmul(
                    PB[bu], uw[slot][:, fc % 8, :], S.h2lo[:, fc, :],
                    start=False, stop=False))
            S.op("tensor", lambda e, bg=bg: e.matmul(
                PB[bg], n128row, cshi, start=False, stop=False))
            S.op("tensor", lambda e, bg=bg: e.matmul(
                PB[bg], n128row, cslo, start=False, stop=True))
            S.op("tensor", lambda e, bu=bu: e.matmul(
                PB[bu], n128row, cshi, start=False, stop=False))
            S.op("tensor", lambda e, bu=bu: e.matmul(
                PB[bu], n128row, cslo, start=False, stop=True), inc=sMM)
            mg = sMM.n
            S.gset(("gw", (hid * 2) % 3), sMM, mg)
            S.gset(("gw", (hid * 2 + 1) % 3), sMM, mg)
            S.gset(("uw", (hid * 2) % 3), sMM, mg)
            S.gset(("uw", (hid * 2 + 1) % 3), sMM, mg)
            S.gwait("scalar", ("sgate",))
            if hid == 0:
                S.gwait("scalar", ("scr",))
                S.gwait("vector", ("scr",))
                S.gwait("vector", ("aofree",))
            S.wait("scalar", sMM, mg)
            as_ = S.op("scalar", lambda e, bg=bg, hid=hid: e.activation(
                out=S.sgate, in_=PB[bg], func=AF.Silu,
                scale=S.sgsu[:, hid:hid + 1]), inc=sA_)
            S.gset(("pb", bg), sA_, as_)
            S.wait("vector", sA_, as_)
            S.gwait("vector", ("guf",))
            S.op("vector", lambda e, bu=bu, hid=hid: e.scalar_tensor_tensor(
                S.guf, PB[bu], S.sgsu[:, 44 + hid:45 + hid], S.sgate,
                ALU.mult, ALU.mult))
            vg1 = S.op("vector", lambda e, hid=hid: e.tensor_copy(
                S.guhi[:, hid, :], S.guf), inc=sV)
            S.gset(("sgate",), sV, vg1)
            S.gset(("pb", bu), sV, vg1)
            vg2 = S.op("vector", lambda e, hid=hid: e.tensor_tensor(
                S.gulo[:, hid, :], S.guf, S.guhi[:, hid, :], ALU.subtract), inc=sV)
            S.gset(("guf",), sV, vg2)
        S.gset(("h2free",), sMM, sMM.n)
        self.gu_done = sV.n
        dw = [S.vb(WREG + k * 2880, 11, 128) for k in range(4)]
        S.wreg_barrier("gpsimd")
        S.wait("tensor", sV, self.gu_done)
        S.gwait("tensor", ("pb", 6))
        for hid in range(HC):
            S.op("tensor", lambda e, hid=hid: e.matmul(
                PB[6][0:1, :], onesbf, S.guhi[:, hid, :],
                start=(hid == 0), stop=False))
            S.op("tensor", lambda e, hid=hid: e.matmul(
                PB[6][0:1, :], onesbf, S.gulo[:, hid, :],
                start=False, stop=(hid == HC - 1)),
                 inc=(sMM if hid == HC - 1 else None))
        mcsd = sMM.n
        S.wait("vector", sMM, mcsd)
        S.op("vector", lambda e: e.tensor_copy(csrow, PB[6][0:1, :]))
        S.op("vector", lambda e: e.tensor_copy(cshi, csrow))
        vcsd = S.op("vector", lambda e: e.tensor_tensor(
            cslo, csrow, cshi, ALU.subtract), inc=sV)
        S.gset(("pb", 6), sV, vcsd)
        S.wait("tensor", sV, vcsd)
        zdone = {}
        for oc in range(16):
            lds = {}
            for qq in range(4):
                slot = (oc * 4 + qq) % 4
                S.gwait("gpsimd", ("dw", slot))
                lds[qq] = (slot, S.dma(
                    "gpsimd", dw[slot],
                    S.d_wd[oc, qq * 1408:(qq + 1) * 1408, :]
                    .rearrange("(a q) w -> q a w", q=128), S.sDW[slot]))
            bk = 4 + oc % 2
            S.gwait("tensor", ("pb", bk))
            for hid in range(HC):
                qq = hid // 11
                slot = (oc * 4 + qq) % 4
                if hid % 11 == 0:
                    S.wait("tensor", S.sDW[lds[qq][0]], lds[qq][1])
                S.op("tensor", lambda e, bk=bk, slot=slot, hid=hid: e.matmul(
                    PB[bk], dw[slot][:, hid % 11, :], S.guhi[:, hid, :],
                    start=(hid == 0), stop=False))
                S.op("tensor", lambda e, bk=bk, slot=slot, hid=hid: e.matmul(
                    PB[bk], dw[slot][:, hid % 11, :], S.gulo[:, hid, :],
                    start=False, stop=False),
                     inc=(sMM if hid % 11 == 10 else None))
                if hid % 11 == 10:
                    S.gset(("dw", slot), sMM, sMM.n)
            S.op("tensor", lambda e, bk=bk: e.matmul(
                PB[bk], n128row, cshi, start=False, stop=False))
            S.op("tensor", lambda e, bk=bk: e.matmul(
                PB[bk], n128row, cslo, start=False, stop=True), inc=sMM)
            md = sMM.n
            S.wait("vector", sMM, md)
            if oc == 0:
                S.gwait("vector", ("h2free",))
            ev = S.op("vector", lambda e, bk=bk, oc=oc: e.tensor_scalar(
                S.zf[:, oc, :], PB[bk], S.mv(MV_SDGB)[:, oc:oc + 1], None,
                ALU.mult), inc=sV)
            S.gset(("pb", bk), sV, ev)
            zdone[oc] = ev
        for tt in range(4):
            gi = 4 * g + tt
            evs = []
            for q in range(4):
                bk = 2 + (tt * 4 + q) % 2
                S.gwait("tensor", ("pb", bk))
                S.wait("tensor", sV, zdone[4 * q + 3])
                for jj in range(4):
                    oc = 4 * q + jj
                    S.op("tensor", lambda e, bk=bk, jj=jj, oc=oc, tt=tt: e.matmul(
                        PB[bk][:, jj * 128:(jj + 1) * 128],
                        S.zf[:, oc, tt * 128:(tt + 1) * 128], ident,
                        is_transpose=True, start=(jj == 0), stop=(jj == 3)),
                         inc=(sMM if jj == 3 else None))
                m = sMM.n
                S.wait("vector", sMM, m)
                ev = S.op("vector", lambda e, bk=bk, tt=tt, q=q: e.tensor_tensor(
                    S.xg[:, tt, q * 512:(q + 1) * 512], PB[bk],
                    S.xg[:, tt, q * 512:(q + 1) * 512], ALU.add), inc=sV)
                S.gset(("pb", bk), sV, ev)
                evs.append(ev)
            S.wait("sync", sV, max(evs))
            so = S.dma("sync", S.d_out[gi * 128:(gi + 1) * 128, :],
                       S.xg[:, tt, :], S.sST[tt])
            S.gset(("xg", tt), S.sST[tt], so)
        S.gset(("zfree",), sMM, sMM.n)
        S.gset(("gufree",), sMM, sMM.n)
        S.gset(("scr",), sMM, sMM.n)
        S.gset(("scr2",), sMM, sMM.n)


def kernel(L0_attn_seq, L0_attn_w_norm, L0_attn_w_qkv, L0_attn_w_out, L0_attn_pm,
           L0_ffn_wq_gate, L0_ffn_s_gate, L0_ffn_wq_up, L0_ffn_s_up,
           L0_ffn_wq_down, L0_ffn_s_down, L0_c, L0_wmod):
    import ml_dtypes

    x = np.asarray(L0_attn_seq, dtype=np.float32)
    wqkv = np.ascontiguousarray(
        np.asarray(L0_attn_w_qkv, dtype=np.float32).reshape(D, 48, 128)
        .transpose(1, 0, 2))
    wout = np.ascontiguousarray(
        np.asarray(L0_attn_w_out, dtype=np.float32).reshape(D, 16, 128)
        .transpose(1, 0, 2))
    wmod = np.ascontiguousarray(
        np.asarray(L0_wmod, dtype=np.float32).reshape(D, 24, 512)
        .transpose(1, 0, 2))
    wg = np.ascontiguousarray(
        np.asarray(L0_ffn_wq_gate, dtype=np.int32).T.reshape(D, HC, 128)
        .transpose(1, 0, 2))
    wu = np.ascontiguousarray(
        np.asarray(L0_ffn_wq_up, dtype=np.int32).T.reshape(D, HC, 128)
        .transpose(1, 0, 2))
    wd = np.ascontiguousarray(
        np.asarray(L0_ffn_wq_down, dtype=np.int32).T.reshape(HID, 16, 128)
        .transpose(1, 0, 2))
    pm = np.ascontiguousarray(np.asarray(L0_attn_pm, dtype=np.float32))
    c = np.asarray(L0_c, dtype=np.float32)
    wn = np.ascontiguousarray(np.asarray(L0_attn_w_norm, dtype=np.float32)
                              .reshape(16, 128))
    sg = np.ascontiguousarray(np.asarray(L0_ffn_s_gate, dtype=np.float32)
                              .reshape(HC, 128))
    su = np.ascontiguousarray(np.asarray(L0_ffn_s_up, dtype=np.float32)
                              .reshape(HC, 128))
    sd = np.ascontiguousarray(np.asarray(L0_ffn_s_down, dtype=np.float32)
                              .reshape(16, 128))
    ident = np.eye(128, dtype=np.float32)

    me = np.full((64, 128), NEG, dtype=np.float32)
    mo = np.full((64, 128), NEG, dtype=np.float32)
    for i in range(64):
        for t in range(64):
            if PMS + t <= i:
                me[i, t] = 0.0
                mo[i, 64 + t] = 0.0
        for p in range(PMS):
            if p <= i:
                me[i, 64 + p] = 0.0
                mo[i, p] = 0.0
    maske = np.ascontiguousarray(np.tile(me, (2, 4))).astype(ml_dtypes.bfloat16)
    masko = np.ascontiguousarray(np.tile(mo, (2, 4))).astype(ml_dtypes.bfloat16)

    b = Builder()
    nc = b.build()

    in_maps = []
    for core in range(8):
        bi = core // 4
        t0 = (core % 4) * TCORE
        in_maps.append({
            "x": np.ascontiguousarray(x[bi, t0:t0 + TCORE, :]),
            "wqkv": wqkv, "wout": wout, "wmod": wmod,
            "wg": wg, "wu": wu, "wd": wd,
            "pmkv": pm,
            "crow": np.ascontiguousarray(c[bi].reshape(16, 128)),
            "wnorm": wn, "sgate": sg, "sup": su, "sdown": sd,
            "ident": ident, "maske": maske, "masko": masko,
        })
    res = run_bass_kernel_spmd(nc, in_maps, core_ids=list(range(8)))
    out = np.empty((2, 8192, D), dtype=np.float32)
    for core in range(8):
        bi = core // 4
        t0 = (core % 4) * TCORE
        out[bi, t0:t0 + TCORE, :] = res.results[core]["out"]
    return out
